# version 07
# baseline (speedup 1.0000x reference)
import os
import numpy as np
import ml_dtypes
from contextlib import ExitStack

import concourse.bass as bass
import concourse.mybir as mybir
from concourse.bass_utils import run_bass_kernel_spmd

F32 = mybir.dt.float32
BF16 = mybir.dt.bfloat16
AF = mybir.ActivationFunctionType
ALU = mybir.AluOpType
AX = mybir.AxisListType

N_CORES = 8
TOK = 4096
N_SUB = 32
N_TILE = 4
SUB_PER_TILE = 8
D = 1024
NHC = 8
QKV = 3072
I_DIM = 2048
N_IC = 16
NE = 8
W = 64
NH = 16
HD = 64
SCALE = 0.125
EPS = 1e-5
_STAGE = int(os.environ.get("KSTAGE", "9999"))


class Em:

    ENGS = ("t", "v", "sc", "g", "sy")

    QLANES = {"ds": 4, "dg": 8, "dc": 3}

    def __init__(self, nc, es):
        self.nc = nc
        self.q = {e: [] for e in self.ENGS}
        self.sems = {}
        self.cnt = {}
        for s in ("pe", "dv", "ac", "po"):
            self.sems[s] = es.enter_context(nc.semaphore("sem_" + s))
            self.cnt[s] = 0
        self.rot = {}
        for qn, nl in self.QLANES.items():
            self.rot[qn] = 0
            for i in range(nl):
                s = f"{qn}{i}"
                self.sems[s] = es.enter_context(nc.semaphore("sem_" + s))
                self.cnt[s] = 0

    def dma(self, eng, fn, deps=()):
        qn = {"sy": "ds", "g": "dg", "sc": "dc"}[eng]
        i = self.rot[qn]
        self.rot[qn] = (i + 1) % self.QLANES[qn]
        lane = f"{qn}{i}"
        deps2 = [(lane, self.cnt[lane])] + list(deps)
        return self.op(eng, fn, deps=deps2, inc=lane, inck=16)

    @staticmethod
    def evmax(evs):
        d = {}
        for (s, c) in evs:
            d[s] = max(d.get(s, 0), c)
        return list(d.items())

    def fence(self, qn):
        return [(f"{qn}{i}", self.cnt[f"{qn}{i}"]) for i in range(self.QLANES[qn])]

    ENG_SEM = {"v": "dv", "sc": "ac", "g": "po"}

    def op(self, eng, fn, deps=(), inc=None, inck=1):
        es = self.ENG_SEM.get(eng)
        if es is not None and inc != es and not str(inc or "").startswith(("ds", "dg", "dc")):
            deps = list(deps) + [(es, self.cnt[es])]
            if inc is None:
                inc = es
        elif es is not None and inc == es:
            deps = list(deps) + [(es, self.cnt[es])]
        deps = [(s, c) for (s, c) in deps if c > 0]
        ev = None
        if inc is not None:
            self.cnt[inc] += inck
            ev = (inc, self.cnt[inc])
        sems = self.sems

        def run(e, fn=fn, deps=deps, inc=inc, inck=inck):
            for (s, c) in deps:
                e.wait_ge(sems[s], c)
            inst = fn(e)
            if inc is not None:
                inst.then_inc(sems[inc], inck)
        self.q[eng].append(run)
        return ev

    def wait(self, eng, deps):
        deps = [(s, c) for (s, c) in deps if c > 0]
        if not deps:
            return
        sems = self.sems

        def run(e, deps=deps):
            for (s, c) in deps:
                e.wait_ge(sems[s], c)
        self.q[eng].append(run)

    def replay(self, blk):
        def mk(lst):
            def body(eng):
                for f in lst:
                    f(eng)
            return body
        blk.tensor(mk(self.q["t"]))
        blk.vector(mk(self.q["v"]))
        blk.scalar(mk(self.q["sc"]))
        blk.gpsimd(mk(self.q["g"]))
        blk.sync(mk(self.q["sy"]))
        self.q = {e: [] for e in self.ENGS}


def build_program(n_sub=N_SUB, n_tile=N_TILE):
    nc = bass.Bass("TRN2", target_bir_lowering=False,
                   detect_race_conditions=not int(os.environ.get("KNORACE", "0")))
    dr = lambda n, sh, dt, k: nc.dram_tensor(n, sh, dt, kind=k).ap()
    xS = dr("xS", [N_SUB, 128, D], F32, "ExternalInput")
    x_tm = dr("x_tm", [TOK, D], F32, "ExternalInput")
    wqkv = dr("wqkv", [D, QKV], F32, "ExternalInput")
    gamma_t = dr("gamma_t", [128, D], F32, "ExternalInput")
    beta_t = dr("beta_t", [128, D], F32, "ExternalInput")
    wproj = dr("wproj", [D, D], F32, "ExternalInput")
    bproj_t = dr("bproj_t", [128, D], F32, "ExternalInput")
    routerT = dr("routerT", [D, NE], F32, "ExternalInput")
    rbias_t = dr("rbias_t", [128, NE], F32, "ExternalInput")
    sgS = dr("sgS", [N_IC, 128, D], F32, "ExternalInput")
    suS = dr("suS", [N_IC, 128, D], F32, "ExternalInput")
    sdS = dr("sdS", [N_IC, 128, D], F32, "ExternalInput")
    wgS = dr("wgS", [NE, N_IC, 128, D], F32, "ExternalInput")
    wuS = dr("wuS", [NE, N_IC, 128, D], F32, "ExternalInput")
    wdS = dr("wdS", [NE, N_IC, 128, D], F32, "ExternalInput")
    ident_in = dr("ident_in", [128, 128], F32, "ExternalInput")
    out_hbm = dr("out", [TOK, D], F32, "ExternalOutput")
    h1fm_s = dr("h1fm_s", [NHC, N_SUB, 128, 128], BF16, "Internal")
    h1tm_s = dr("h1tm_s", [TOK, D], F32, "Internal")

    es = ExitStack()
    T = lambda n, sh, dt: es.enter_context(nc.sbuf_tensor(n, sh, dt))
    em = Em(nc, es)

    ident = T("ident", [128, 128], F32)
    ident_bf = T("ident_bf", [128, 128], BF16)
    eps_t = T("eps_t", [128, 1], F32)
    router_bf = T("router_bf", [128, NHC * NE], BF16)
    rbias = T("rbias", [128, NE], F32)
    ps_a = es.enter_context(nc.psum_tensor("ps_a", [128, 2048], F32))
    ps_b = es.enter_context(nc.psum_tensor("ps_b", [128, 2048], F32))
    ps_b_hi_bf = ps_b[:, 1024:2048].bitcast(BF16)
    ps_b_lo_bf = ps_b[:, 0:1024].bitcast(BF16)

    phA = ExitStack()
    TA = lambda n, sh, dt: phA.enter_context(nc.sbuf_tensor(n, sh, dt))
    wqkv_bf = TA("wqkv_bf", [128, NHC * QKV], BF16)
    wproj_bf = TA("wproj_bf", [128, NHC * D], BF16)
    gamma_qk = TA("gamma_qk", [128, 2048], F32)
    beta_qk = TA("beta_qk", [128, 2048], F32)
    bproj = TA("bproj", [128, D], F32)
    gamma_sb = TA("gamma_sb", [128, D], F32)
    beta_sb = TA("beta_sb", [128, D], F32)
    xfm_bf = [TA(f"xfm_bf{i}", [128, D], BF16) for i in range(2)]
    xtm_f = [TA(f"xtm_f{i}", [128, D], F32) for i in range(2)]
    qn0 = TA("qn0", [128, 2048], F32)
    sq_sb = TA("sq_sb", [128, 2048], F32)
    qng = TA("qng", [128, 2048], F32)
    qnk_bf = TA("qnk_bf", [128, 2048], BF16)
    v_bf = TA("v_bf", [128, D], BF16)
    v_x = TA("v_x", [128, D], BF16)
    qnT_bf = TA("qnT_bf", [128, D], BF16)
    knT_bf = TA("knT_bf", [128, D], BF16)
    exp_in = TA("exp_in", [64, 2048], F32)
    expv = TA("expv", [64, 2048], F32)
    S_n = TA("S_n", [64, 2048], F32)
    ST_bf = TA("ST_bf", [128, D], BF16)
    o_bf = TA("o_bf", [128, D], BF16)
    r_f = TA("r_f", [128, D], F32)
    rc_f = TA("rc_f", [128, D], F32)
    h1_tm = TA("h1_tm", [128, D], F32)
    h1_bf = TA("h1_bf", [128, D], BF16)
    h1T_bf = TA("h1T_bf", [128, D], BF16)
    stat = TA("stat", [128, 32], F32)
    stat2 = TA("stat2", [128, 32], F32)
    negmu = TA("negmu", [128, 32], F32)
    mu2 = TA("mu2", [128, 32], F32)
    var_t = TA("var_t", [128, 32], F32)
    rs_t = TA("rs_t", [128, 32], F32)
    negmax = TA("negmax", [64, 32], F32)
    sums = TA("sums", [64, 32], F32)
    rsum = TA("rsum", [64, 32], F32)
    ln1 = TA("ln1", [128, 8], F32)


    ev_wqkv = em.dma("g", lambda g: g.dma_start(
        out=wqkv_bf[:].rearrange("p (c n) -> p c n", c=NHC),
        in_=wqkv.rearrange("(c p) n -> p c n", p=128)))
    ev_wproj = em.dma("g", lambda g: g.dma_start(
        out=wproj_bf[:].rearrange("p (c n) -> p c n", c=NHC),
        in_=wproj.rearrange("(c p) n -> p c n", p=128)))
    ev_router = em.dma("g", lambda g: g.dma_start(
        out=router_bf[:].rearrange("p (c e) -> p c e", c=NHC),
        in_=routerT.rearrange("(c p) e -> p c e", p=128)))
    ev_ident = em.dma("sy", lambda sy: sy.dma_start(out=ident[:], in_=ident_in))
    ev_identbf = em.dma("g", lambda g: g.dma_start(out=ident_bf[:], in_=ident_in))
    ev_gam = em.dma("sy", lambda sy: sy.dma_start(out=gamma_sb[:], in_=gamma_t))
    ev_bet = em.dma("sy", lambda sy: sy.dma_start(out=beta_sb[:], in_=beta_t))
    ev_bpj = em.dma("sy", lambda sy: sy.dma_start(out=bproj[:], in_=bproj_t))
    ev_rbias = em.dma("sy", lambda sy: sy.dma_start(out=rbias[:], in_=rbias_t))
    ev_eps = em.op("g", lambda g: g.memset(eps_t[:], EPS), inc="po")
    ev_cq = em.op("v", lambda v: v.tensor_scalar(gamma_qk[:, 0:1024], gamma_sb[:], SCALE, None, ALU.mult),
                  deps=[ev_gam], inc="dv")
    em.op("v", lambda v: v.tensor_copy(gamma_qk[:, 1024:2048], gamma_sb[:]), inc="dv")
    em.op("v", lambda v: v.tensor_scalar(beta_qk[:, 0:1024], beta_sb[:], SCALE, None, ALU.mult),
          deps=[ev_bet], inc="dv")
    ev_consts = em.op("v", lambda v: v.tensor_copy(beta_qk[:, 1024:2048], beta_sb[:]), inc="dv")

    Z = (None, 0)
    none_ev = ("dv", 0)
    ev_pe_qkv = [none_ev] * N_SUB
    ev_pe_qnT = [none_ev] * N_SUB
    ev_pe_scores = [none_ev] * N_SUB
    ev_pe_ST = [none_ev] * N_SUB
    ev_pe_attnv = [none_ev] * N_SUB
    ev_pe_proj = [none_ev] * N_SUB
    ev_pe_h1T = [none_ev] * N_SUB
    ev_v_center = [none_ev] * N_SUB
    ev_v_ssum = [none_ev] * N_SUB
    ev_v_vevac = [none_ev] * N_SUB
    ev_v_qnTevac = [none_ev] * N_SUB
    ev_v_expin = [none_ev] * N_SUB
    ev_v_Sn = [none_ev] * N_SUB
    ev_v_STevac = [none_ev] * N_SUB
    ev_v_oevac = [none_ev] * N_SUB
    ev_v_prE = [none_ev] * N_SUB
    ev_v_resid = [none_ev] * N_SUB
    ev_v_h1 = [none_ev] * N_SUB
    ev_v_h1bf = [none_ev] * N_SUB
    ev_v_h1Tevac = [none_ev] * N_SUB
    ev_a_sq = [none_ev] * N_SUB
    ev_a_rs = [none_ev] * N_SUB
    ev_a_exp = [none_ev] * N_SUB
    ev_a_sq1 = [none_ev] * N_SUB
    ev_a_rs1 = [none_ev] * N_SUB
    ev_g_qnk = [none_ev] * N_SUB
    ev_ds_vx = [none_ev] * N_SUB
    ev_ds_vx1 = [none_ev] * N_SUB
    ev_ds_xtm = [none_ev] * N_SUB
    ev_dg_xfm = [none_ev] * N_SUB
    ev_dc_h1T = [none_ev] * N_SUB
    ev_dc_h1tm = [none_ev] * N_SUB

    for st in range(n_sub):
        b = st % 2
        P = lambda i: (st - i) if st >= i else None

        deps = [ev_pe_qkv[st - 2], ev_dg_xfm[st - 2]] if st >= 2 else []
        ev_dg_xfm[st] = em.dma("g", lambda g, st=st, b=b: g.dma_start(
            out=xfm_bf[b][:], in_=xS[st]), deps=deps)
        deps = [ev_v_resid[st - 2], ev_ds_xtm[st - 2]] if st >= 2 else []
        ev_ds_xtm[st] = em.dma("sy", lambda sy, st=st, b=b: sy.dma_start(
            out=xtm_f[b][:], in_=x_tm[st * 128:(st + 1) * 128, :]), deps=deps)

        pe_deps = [ev_wqkv, ev_dg_xfm[st]]
        if st >= 1:
            pe_deps += [ev_v_expin[st - 1]]
        em.wait("t", pe_deps)

        def qk_mms(t, b=b):
            last = None
            for hc in range(NHC):
                for nn in range(4):
                    last = t.matmul(ps_a[:, nn * 512:(nn + 1) * 512],
                             xfm_bf[b][:, hc * 128:(hc + 1) * 128],
                             wqkv_bf[:, hc * QKV + nn * 512: hc * QKV + (nn + 1) * 512],
                             start=(hc == 0), stop=(hc == NHC - 1))
            return last
        em.op("t", qk_mms, inc="pe")
        ev_pe_qk = ("pe", em.cnt["pe"])

        def v_mms(t, b=b):
            last = None
            for hc in range(NHC):
                for nn in range(2):
                    last = t.matmul(ps_b[:, 1024 + nn * 512:1024 + (nn + 1) * 512],
                             xfm_bf[b][:, hc * 128:(hc + 1) * 128],
                             wqkv_bf[:, hc * QKV + 2048 + nn * 512: hc * QKV + 2048 + (nn + 1) * 512],
                             start=(hc == 0), stop=(hc == NHC - 1))
            return last
        if st >= 1:
            em.wait("t", [ev_v_prE[st - 1]])
        em.op("t", v_mms, inc="pe")
        ev_pe_qkv[st] = ("pe", em.cnt["pe"])

        if _STAGE <= 1:
            continue
        ps_qk3 = ps_a[:].rearrange("p (s k) -> p s k", s=32)
        dv_deps = [ev_pe_qk]
        if st >= 1:
            dv_deps.append(ev_a_sq[st - 1])
        ev_st1 = em.op("v", lambda v: v.tensor_reduce(
            stat[:].unsqueeze(2), ps_qk3, AX.X, ALU.add, negate=True),
            deps=dv_deps, inc="dv")
        if _STAGE <= 2:
            continue
        ev_a_sq[st] = em.op("sc", lambda sc: sc.activation(sq_sb[:], ps_a[:], AF.Square),
                            deps=[ev_pe_qk, ev_st1] + ([ev_v_ssum[st - 1]] if st >= 1 else []),
                            inc="ac")
        ev_v_ssum[st] = em.op("v", lambda v: v.tensor_reduce(
            stat2[:].unsqueeze(2),
            sq_sb[:].rearrange("p (s k) -> p s k", s=32), AX.X, ALU.add),
            deps=[ev_a_sq[st]], inc="dv")
        if _STAGE <= 3:
            continue
        em.op("v", lambda v: v.tensor_scalar(negmu[:], stat[:], 1.0 / 64, None, ALU.mult),
              deps=([ev_a_rs[st - 1]] if st >= 1 else []), inc="dv")
        em.op("v", lambda v: v.tensor_mul(mu2[:], negmu[:], negmu[:]), inc="dv")
        em.op("v", lambda v: v.tensor_scalar(var_t[:], stat2[:], 1.0 / 64, None, ALU.mult), inc="dv")
        ev_var = em.op("v", lambda v: v.tensor_sub(var_t[:], var_t[:], mu2[:]), inc="dv")
        if _STAGE <= 4:
            continue
        ev_lnv = em.op("sc", lambda sc: sc.activation(var_t[:], var_t[:], AF.Ln, bias=eps_t[:, 0:1]),
                       deps=[ev_var, ev_eps], inc="ac")
        ev_a_rs[st] = em.op("sc", lambda sc: sc.activation(rs_t[:], var_t[:], AF.Exp, scale=-0.5),
                            deps=([ev_v_center[st - 1]] if st >= 1 else []), inc="ac")
        if _STAGE <= 5:
            continue
        nmu_bc = negmu[:].unsqueeze(2).broadcast_to((128, 32, 64))
        rs_bc = rs_t[:].unsqueeze(2).broadcast_to((128, 32, 64))
        q3 = qn0[:].rearrange("p (s k) -> p s k", s=32)
        em.op("v", lambda v: v.tensor_tensor(q3, ps_qk3, nmu_bc, ALU.add),
              deps=[ev_st1, ev_a_rs[st]] + ([ev_g_qnk[st - 1]] if st >= 1 else []), inc="dv")
        ev_v_center[st] = em.op("v", lambda v: v.tensor_tensor(q3, q3, rs_bc, ALU.mult), inc="dv")
        if _STAGE <= 6:
            continue
        dep_prev = [ev_pe_qnT[st - 1]] if st >= 1 else []
        em.op("g", lambda g: g.tensor_mul(qng[:, 0:1024], qn0[:, 0:1024], gamma_qk[:, 0:1024]),
              deps=[ev_v_center[st], ev_consts] + dep_prev, inc="po")
        ev_g_q = em.op("g", lambda g: g.tensor_add(qnk_bf[:, 0:1024], qng[:, 0:1024], beta_qk[:, 0:1024]), inc="po")
        em.op("v", lambda v: v.tensor_mul(qng[:, 1024:2048], qn0[:, 1024:2048], gamma_qk[:, 1024:2048]),
              deps=[ev_consts] + dep_prev, inc="dv")
        ev_v_k = em.op("v", lambda v: v.tensor_add(qnk_bf[:, 1024:2048], qng[:, 1024:2048], beta_qk[:, 1024:2048]), inc="dv")
        ev_g_qnk[st] = ev_g_q
        ev_v_qnk = ev_v_k

        if _STAGE <= 7:
            continue
        ev_v_vevac[st] = em.op("v", lambda v: v.tensor_copy(v_bf[:], ps_b[:, 1024:2048]),
                               deps=[ev_pe_qkv[st]] + ([ev_ds_vx[st - 1], ev_ds_vx1[st - 1]] if st >= 1 else []),
                               inc="dv")
        ev_vx1 = em.dma("sy", lambda sy: sy.dma_start(out=v_x[0:64, :], in_=v_bf[64:128, :]),
                        deps=[ev_v_vevac[st]] + ([ev_pe_attnv[st - 1], ev_ds_vx1[st - 1]] if st >= 1 else []))
        ev_vx2 = em.dma("sy", lambda sy: sy.dma_start(out=v_x[64:128, :], in_=v_bf[0:64, :]),
                        deps=([ev_ds_vx[st - 1]] if st >= 1 else []))
        ev_ds_vx[st] = ev_vx2
        ev_ds_vx1[st] = ev_vx1

        if _STAGE <= 8:
            continue
        t_deps = [ev_g_qnk[st], ev_v_qnk, ev_ident, ev_identbf, ev_v_vevac[st]]
        em.wait("t", t_deps)

        def qnT_mms(t):
            last = None
            for c in range(2 * NHC):
                last = t.transpose(ps_b_hi_bf[:, c * 128:(c + 1) * 128],
                            qnk_bf[:, c * 128:(c + 1) * 128],
                            ident_bf[:])
            return last
        em.op("t", qnT_mms, inc="pe")
        ev_pe_qnT[st] = ("pe", em.cnt["pe"])

        em.op("v", lambda v: v.tensor_copy(qnT_bf[:], ps_b_hi_bf[:, 0:1024]),
              deps=[ev_pe_qnT[st]] + ([ev_pe_scores[st - 1]] if st >= 1 else []), inc="dv")
        ev_v_qnTevac[st] = em.op("v", lambda v: v.tensor_copy(knT_bf[:], ps_b_hi_bf[:, 1024:2048]),
                                 inc="dv")

        if _STAGE <= 9:
            continue
        em.wait("t", [ev_v_qnTevac[st], ev_v_center[st], ev_a_sq[st]])

        def score_mms_par(par):
            def f(t):
                last = None
                for w in range(2):
                    for h in range(par, NH, 2):
                        c, hp = h // 2, (h % 2) * 64
                        s = w * NH + h
                        last = t.matmul(ps_a[0:64, s * 64:(s + 1) * 64],
                                 qnT_bf[hp:hp + 64, c * 128 + w * 64: c * 128 + w * 64 + 64],
                                 knT_bf[hp:hp + 64, c * 128 + w * 64: c * 128 + w * 64 + 64],
                                 start=True, stop=True, tile_position=(hp, 0))
                return last
            return f
        em.op("t", score_mms_par(0), inc="pe")
        em.wait("t", [("pe", em.cnt["pe"])])
        em.op("t", score_mms_par(1), inc="pe")
        ev_pe_scores[st] = ("pe", em.cnt["pe"])

        if _STAGE <= 10:
            continue
        sc3 = ps_a[0:64, :].rearrange("p (s k) -> p s k", s=32)
        ev_nm = em.op("v", lambda v: v.tensor_reduce(
            negmax[:].unsqueeze(2), sc3, AX.X, ALU.max, negate=True),
            deps=[ev_pe_scores[st]], inc="dv")
        nmax_bc = negmax[:].unsqueeze(2).broadcast_to((64, 32, 64))
        ev_v_expin[st] = em.op("v", lambda v: v.tensor_tensor(
            exp_in[:].rearrange("p (s k) -> p s k", s=32), sc3, nmax_bc, ALU.add),
            deps=[ev_nm] + ([ev_a_exp[st - 1]] if st >= 1 else []), inc="dv")
        ev_a_exp[st] = em.op("sc", lambda sc: sc.activation(expv[:], exp_in[:], AF.Exp),
                             deps=[ev_v_expin[st]] + ([ev_v_Sn[st - 1]] if st >= 1 else []),
                             inc="ac")
        em.op("v", lambda v: v.tensor_reduce(
            sums[:].unsqueeze(2),
            expv[:].rearrange("p (s k) -> p s k", s=32), AX.X, ALU.add),
            deps=[ev_a_exp[st]], inc="dv")
        em.op("v", lambda v: v.reciprocal(rsum[:], sums[:]), inc="dv")
        rsum_bc = rsum[:].unsqueeze(2).broadcast_to((64, 32, 64))
        ev_v_Sn[st] = em.op("v", lambda v: v.tensor_tensor(
            S_n[:].rearrange("p (s k) -> p s k", s=32),
            expv[:].rearrange("p (s k) -> p s k", s=32), rsum_bc, ALU.mult),
            deps=[] if st == 0 else [ev_pe_ST[st - 1]], inc="dv")

        if _STAGE <= 11:
            continue
        em.wait("t", [ev_v_Sn[st]] + ([ev_v_h1Tevac[st - 1]] if st >= 1 else []))

        def st_mms(t):
            last = None
            for j in range(16):
                last = t.transpose(ps_b[:, j * 64:(j + 1) * 64],
                            S_n[0:64, j * 128:(j + 1) * 128],
                            ident[0:64, 0:64])
            return last
        em.op("t", st_mms, inc="pe")
        ev_pe_ST[st] = ("pe", em.cnt["pe"])

        ev_v_STevac[st] = em.op("v", lambda v: v.tensor_copy(ST_bf[:], ps_b[:, 0:1024]),
                                deps=[ev_pe_ST[st]] + ([ev_pe_attnv[st - 1]] if st >= 1 else []),
                                inc="dv")

        if _STAGE <= 12:
            continue
        em.wait("t", [ev_v_STevac[st], ev_ds_vx1[st], ev_ds_vx[st]])

        def av_mms_par(par):
            def f(t):
                last = None
                for w in range(2):
                    for h in range(par, NH, 2):
                        c, hp = h // 2, (h % 2) * 64
                        j = w * 8 + h // 2
                        vsrc = v_bf if (w == hp // 64) else v_x
                        last = t.matmul(ps_b[hp:hp + 64, c * 128 + w * 64: c * 128 + w * 64 + 64],
                                 vsrc[hp:hp + 64, h * 64:(h + 1) * 64],
                                 ST_bf[hp:hp + 64, j * 64:(j + 1) * 64],
                                 start=True, stop=True, tile_position=(hp, hp))
                return last
            return f
        em.op("t", av_mms_par(0), inc="pe")
        em.wait("t", [("pe", em.cnt["pe"])])
        em.op("t", av_mms_par(1), inc="pe")
        ev_pe_attnv[st] = ("pe", em.cnt["pe"])

        ev_v_oevac[st] = em.op("v", lambda v: v.tensor_copy(o_bf[:], ps_b[:, 0:1024]),
                               deps=[ev_pe_attnv[st]] + ([ev_pe_proj[st - 1]] if st >= 1 else []),
                               inc="dv")

        if _STAGE <= 13:
            continue
        em.wait("t", [ev_v_oevac[st], ev_wproj, ev_v_qnTevac[st]])

        def proj_mms(t):
            last = None
            for c in range(NHC):
                for nn in range(2):
                    last = t.matmul(ps_b[:, 1024 + nn * 512:1024 + (nn + 1) * 512],
                             o_bf[:, c * 128:(c + 1) * 128],
                             wproj_bf[:, c * D + nn * 512: c * D + (nn + 1) * 512],
                             start=(c == 0), stop=(c == NHC - 1))
            return last
        em.op("t", proj_mms, inc="pe")
        ev_pe_proj[st] = ("pe", em.cnt["pe"])

        if _STAGE <= 14:
            continue
        ev_r0 = em.op("v", lambda v: v.scalar_tensor_tensor(
            r_f[:], ps_b[:, 1024:2048], 1.0, bproj[:], ALU.mult, ALU.add),
            deps=[ev_pe_proj[st], ev_bpj] + ([ev_a_sq1[st - 1]] if st >= 1 else []), inc="dv")
        ev_v_resid[st] = em.op("v", lambda v, b=b: v.tensor_add(r_f[:], r_f[:], xtm_f[b][:]),
                               deps=[ev_ds_xtm[st]], inc="dv")
        ev_prE_ = em.op("v", lambda v: v.tensor_reduce(ln1[:, 0:1], r_f[:], AX.X, ALU.add, negate=True),
                        inc="dv")
        ev_v_prE[st] = ev_r0
        ev_nm1 = em.op("v", lambda v: v.tensor_scalar(ln1[:, 1:2], ln1[:, 0:1], 1.0 / D, None, ALU.mult),
                       inc="dv")
        ev_rc = em.op("v", lambda v: v.tensor_scalar(rc_f[:], r_f[:], ln1[:, 1:2], None, ALU.add),
                      deps=[ev_nm1], inc="dv")
        ev_a_sq1[st] = em.op("sc", lambda sc: sc.activation(sq_sb[:, 0:1024], rc_f[:], AF.Square,
                                                            accum_out=ln1[:, 2:3]),
                             deps=[ev_rc, ev_v_ssum[st]], inc="ac")
        em.op("sc", lambda sc: sc.activation(ln1[:, 3:4], ln1[:, 2:3], AF.Ln,
                                             scale=1.0 / D, bias=eps_t[:, 0:1]), inc="ac")
        ev_a_rs1[st] = em.op("sc", lambda sc: sc.activation(ln1[:, 4:5], ln1[:, 3:4], AF.Exp,
                                                            scale=-0.5),
                             deps=([ev_v_h1[st - 1]] if st >= 1 else []), inc="ac")
        ev_v_h1[st] = em.op("v", lambda v: v.tensor_scalar(h1_tm[:], rc_f[:], ln1[:, 4:5], None, ALU.mult),
                            deps=[ev_a_rs1[st]] + ([ev_dc_h1tm[st - 1]] if st >= 1 else []),
                            inc="dv")
        ev_v_h1bf[st] = em.op("v", lambda v: v.tensor_copy(h1_bf[:], h1_tm[:]),
                              deps=([ev_pe_h1T[st - 1]] if st >= 1 else []), inc="dv")

        if _STAGE <= 15:
            continue
        em.wait("t", [ev_v_h1bf[st], ev_v_oevac[st]])

        def h1T_mms(t):
            last = None
            for c in range(NHC):
                last = t.transpose(ps_b_lo_bf[:, c * 128:(c + 1) * 128],
                            h1_bf[:, c * 128:(c + 1) * 128],
                            ident_bf[:])
            return last
        em.op("t", h1T_mms, inc="pe")
        ev_pe_h1T[st] = ("pe", em.cnt["pe"])

        ev_v_h1Tevac[st] = em.op("v", lambda v: v.tensor_copy(h1T_bf[:], ps_b_lo_bf[:, 0:1024]),
                                 deps=[ev_pe_h1T[st]] + ([ev_dc_h1T[st - 1]] if st >= 1 else []),
                                 inc="dv")

        if _STAGE <= 16:
            continue
        _KSTORE = int(os.environ.get("KSTORE", "3"))
        _h1q = "sy" if int(os.environ.get("KH1SY", "0")) else "sc"
        if _KSTORE & 2:
            ev_dc_h1T[st] = em.dma(_h1q, lambda sc, st=st: sc.dma_start(
                out=h1fm_s[:, st].rearrange("c p t -> p c t"),
                in_=h1T_bf[:].rearrange("p (c t) -> p c t", c=NHC)),
                deps=[ev_v_h1Tevac[st]])
        if _KSTORE & 1:
            ev_dc_h1tm[st] = em.dma(_h1q, lambda sc, st=st: sc.dma_start(
                out=h1tm_s[st * 128:(st + 1) * 128, :], in_=h1_tm[:]),
                deps=[ev_v_h1[st]])

    endA = ([(s, em.cnt[s]) for s in ("pe", "dv", "ac", "po")]
            + em.fence("ds") + em.fence("dg") + em.fence("dc"))
    fence_dc_A = em.fence("dc")

    with nc.Block() as blkA:
        em.replay(blkA)
    phA.close()

    phB = ExitStack()
    TB = lambda n, sh, dt: phB.enter_context(nc.sbuf_tensor(n, sh, dt))
    h1fm_t = [TB(f"h1fm_t{i}", [128, NHC * 1024], BF16) for i in range(2)]
    h1tm_t = TB("h1tm_t", [128, 8 * 1024], F32)
    acc = TB("acc", [128, 8 * 1024], F32)
    h_e = TB("h_e", [128, N_IC * 1024], BF16)
    wd_all = TB("wd_all", [128, N_IC * 1024], BF16)
    wg_ring = [TB(f"wg_ring{i}", [128, 1024], BF16) for i in range(3)]
    wu_ring = [TB(f"wu_ring{i}", [128, 1024], BF16) for i in range(3)]
    sg_bf = TB("sg_bf", [128, 1024], BF16)
    lg_t = TB("lg_t", [128, 8 * NE], F32)
    srt_t = TB("srt_t", [128, NE], F32)
    e12 = TB("e12", [128, 2], F32)
    p12 = TB("p12", [128, 2], F32)
    pn = TB("pn", [128, 2], F32)
    wn_t = TB("wn_t", [128, 2], F32)
    m1_t = TB("m1_t", [128, NE], F32)
    m2_t = TB("m2_t", [128, NE], F32)
    mt_t = TB("mt_t", [128, NE], F32)
    w_t = TB("w_t", [128, 8 * NE], F32)
    r2_f = TB("r2_f", [128, 1024], F32)
    rc2_f = TB("rc2_f", [128, 1024], F32)
    sq2 = TB("sq2", [128, 1024], F32)
    ln2 = TB("ln2", [128, 8], F32)
    out_t = [TB(f"out_t{i}", [128, 1024], F32) for i in range(2)]

    em.wait("t", endA)
    em.wait("v", endA)
    em.wait("sc", endA)
    em.wait("g", endA)
    em.wait("sy", endA)

    ev_v_LN2 = [[none_ev] * 8 for _ in range(N_TILE)]
    ev_out_dma = [[none_ev] * 8 for _ in range(N_TILE)]
    ev_v_acc = {}
    ev_pe_g = {}
    ev_pe_u = {}
    ev_a_silu = {}
    ev_v_hmul = {}
    ev_pe_y = {}
    ev_dg_slab = {}
    ev_pe_router_last = none_ev
    ev_pe_u_done = [none_ev] * N_TILE
    ev_pe_y_done = [none_ev] * N_TILE
    ev_h1fm_prev = [none_ev] * NHC
    ev_h1fm_tile = [None] * (N_TILE + 1)
    ev_h1tm_prev = [none_ev] * 8
    ev_dg_gslab = {}
    ev_dg_uslab = {}
    ev_wd_prev = [none_ev] * N_IC

    _KPB = int(os.environ.get("KPB", "0"))
    for Ti in range(n_tile):
        def emit_h1fm_loads(Tl):
            buf = h1fm_t[Tl % 2]
            ld = [endA[0]] if Tl == 0 else [ev_pe_u_done[Tl - 2]] if Tl >= 2 else []
            if Tl == 0:
                ld = ld + fence_dc_A
            evs = []
            for c in range(NHC):
                evs.append(em.dma("sy", lambda sy, Tl=Tl, c=c, buf=buf: sy.dma_start(
                    out=buf[:, c * 1024:(c + 1) * 1024].rearrange("p (s t) -> p s t", s=8),
                    in_=h1fm_s[c, Tl * 8:(Tl + 1) * 8].rearrange("s p t -> p s t")),
                    deps=(ld if c == 0 else []) + [ev_h1fm_prev[c]]))
                ev_h1fm_prev[c] = evs[-1]
            ev_h1fm_tile[Tl] = Em.evmax(evs)
        if Ti == 0:
            emit_h1fm_loads(0)
        ev_h1fm_all = ev_h1fm_tile[Ti]
        h1fm_cur = h1fm_t[Ti % 2]
        ev_h1tm = []
        for tc in range(8):
            if _KPB & 2 and Ti >= 1:
                ev_h1tm.append(none_ev)
                continue
            dps = [] if Ti == 0 else [ev_v_LN2[Ti - 1][tc]]
            if Ti == 0 and tc == 0:
                dps = fence_dc_A
            ev_h1tm.append(em.dma("sy", lambda sy, Ti=Ti, tc=tc: sy.dma_start(
                out=h1tm_t[:, tc * 1024:(tc + 1) * 1024],
                in_=h1tm_s[Ti * 1024 + tc * 128: Ti * 1024 + (tc + 1) * 128, :]),
                deps=dps + [ev_h1tm_prev[tc]]))
            ev_h1tm_prev[tc] = ev_h1tm[-1]

        ev_v_w = none_ev
        ev_lgread = none_ev
        for tc in range(8):
            rt_deps = ev_h1fm_all + [ev_router, ev_lgread]
            if Ti > 0 and tc == 0:
                rt_deps.append(ev_v_acc.get((Ti - 1, 8, 6), none_ev))
            em.wait("t", rt_deps)

            def rt_mms(t, Ti=Ti, tc=tc, hb=h1fm_cur):
                last = None
                for c in range(NHC):
                    last = t.matmul(ps_b[:, 0:NE],
                             hb[:, c * 1024 + tc * 128: c * 1024 + (tc + 1) * 128],
                             router_bf[:, c * NE:(c + 1) * NE],
                             start=(c == 0), stop=(c == NHC - 1))
                return last
            em.op("t", rt_mms, inc="pe")
            ev_rt = ("pe", em.cnt["pe"])
            lg = lg_t[:, tc * NE:(tc + 1) * NE]
            wcol = w_t[:, tc * NE:(tc + 1) * NE]
            ev1 = em.op("v", lambda v, lg=lg: v.scalar_tensor_tensor(
                lg, ps_b[:, 0:NE], 1.0, rbias[:], ALU.mult, ALU.add),
                deps=[ev_rt, ev_rbias], inc="dv")
            ev_lgread = ev1
            ev2 = em.op("v", lambda v, lg=lg: v.max(srt_t[:], lg), deps=[ev1], inc="dv")
            ev3 = em.op("sc", lambda sc: sc.activation(e12[:], srt_t[:, 0:2], AF.Exp, scale=-1.0),
                        deps=[ev2], inc="ac")
            ev4 = em.op("v", lambda v: v.tensor_scalar(p12[:], e12[:], 1.0, None, ALU.add),
                        deps=[ev3], inc="dv")
            ev5 = em.op("v", lambda v: v.reciprocal(p12[:], p12[:]), inc="dv")
            ev6 = em.op("v", lambda v: v.tensor_add(pn[:, 0:1], p12[:, 0:1], p12[:, 1:2]), inc="dv")
            ev7 = em.op("v", lambda v: v.reciprocal(pn[:, 1:2], pn[:, 0:1]), deps=[ev6], inc="dv")
            ev8 = em.op("v", lambda v: v.tensor_scalar(wn_t[:], p12[:], pn[:, 1:2], None, ALU.mult),
                        deps=[ev7], inc="dv")
            ev9 = em.op("v", lambda v, lg=lg: v.tensor_scalar(m1_t[:], lg, srt_t[:, 0:1], None, ALU.is_equal),
                        deps=[ev2], inc="dv")
            em.op("v", lambda v, lg=lg: v.tensor_scalar(m2_t[:], lg, srt_t[:, 1:2], None, ALU.is_equal),
                  inc="dv")
            em.op("v", lambda v: v.tensor_mul(mt_t[:], m1_t[:], m2_t[:]), inc="dv")
            em.op("v", lambda v: v.tensor_sub(m2_t[:], m2_t[:], mt_t[:]), inc="dv")
            ev_w1 = em.op("v", lambda v, wcol=wcol: v.tensor_scalar(wcol, m1_t[:], wn_t[:, 0:1], None, ALU.mult),
                          deps=[ev8], inc="dv")
            ev_v_w = em.op("v", lambda v, wcol=wcol: v.scalar_tensor_tensor(
                wcol, m2_t[:], wn_t[:, 1:2], wcol, ALU.mult, ALU.add),
                deps=[ev_w1], inc="dv")
        ev_pe_router_last = ("pe", em.cnt["pe"])
        ev_router_w = ev_v_w
        if Ti + 1 < n_tile:
            emit_h1fm_loads(Ti + 1)

        for e in range(9):
            gsrc = sgS if e == 0 else wgS[e - 1]
            usrc = suS if e == 0 else wuS[e - 1]
            dsrc = sdS if e == 0 else wdS[e - 1]
            gi0 = (Ti * 9 + e) * N_IC

            wd_dep = []
            if (Ti, e) != (0, 0):
                prev = (Ti, e - 1) if e > 0 else (Ti - 1, 8)
                wd_dep = [ev_pe_y.get((prev[0], prev[1], 7), none_ev)]
            ev_wd = []

            for ic in range(N_IC):
                gi = gi0 + ic
                slot = gi % 3
                ring_dep = []
                if gi >= 3:
                    ring_dep = [ev_pe_u.get(gi - 3, none_ev)]
                ev_g_slab = em.dma("g", lambda g, gsrc=gsrc, ic=ic, slot=slot: g.dma_start(
                    out=wg_ring[slot][:], in_=gsrc[ic]),
                    deps=ring_dep + [ev_dg_gslab.get(gi - 3, none_ev)])
                ev_u_slab = em.dma("g", lambda g, usrc=usrc, ic=ic, slot=slot: g.dma_start(
                    out=wu_ring[slot][:], in_=usrc[ic]),
                    deps=ring_dep + [ev_dg_uslab.get(gi - 3, none_ev)])
                ev_dg_gslab[gi] = ev_g_slab
                ev_dg_uslab[gi] = ev_u_slab
                ev_wd.append(em.dma("g", lambda g, dsrc=dsrc, ic=ic: g.dma_start(
                    out=wd_all[:, ic * 1024:(ic + 1) * 1024], in_=dsrc[ic]),
                    deps=(wd_dep if ic == 0 else []) + [ev_wd_prev[ic]]))
                ev_wd_prev[ic] = ev_wd[-1]

                pe_deps = [ev_g_slab] + ev_h1fm_all
                if gi >= 1:
                    pe_deps.append(ev_a_silu.get(gi - 1, none_ev))
                if Ti == 0 and e == 0 and ic == 0:
                    pe_deps.append(ev_pe_router_last)
                em.wait("t", pe_deps)

                def g_mms(t, slot=slot, hb=h1fm_cur):
                    last = None
                    for hc in range(NHC):
                        for nn in range(2):
                            last = t.matmul(ps_a[:, nn * 512:(nn + 1) * 512],
                                     wg_ring[slot][:, hc * 128:(hc + 1) * 128],
                                     hb[:, hc * 1024 + nn * 512: hc * 1024 + (nn + 1) * 512],
                                     start=(hc == 0), stop=(hc == NHC - 1))
                    return last
                em.op("t", g_mms, inc="pe")
                ev_pe_g[gi] = ("pe", em.cnt["pe"])

                pe_deps = [ev_u_slab]
                if gi >= 1:
                    pe_deps.append(ev_v_hmul.get(gi - 1, none_ev))
                em.wait("t", pe_deps)

                def u_mms(t, slot=slot, hb=h1fm_cur):
                    last = None
                    for hc in range(NHC):
                        for nn in range(2):
                            last = t.matmul(ps_a[:, 1024 + nn * 512:1024 + (nn + 1) * 512],
                                     wu_ring[slot][:, hc * 128:(hc + 1) * 128],
                                     hb[:, hc * 1024 + nn * 512: hc * 1024 + (nn + 1) * 512],
                                     start=(hc == 0), stop=(hc == NHC - 1))
                    return last
                em.op("t", u_mms, inc="pe")
                ev_pe_u[gi] = ("pe", em.cnt["pe"])

                ev_a_silu[gi] = em.op("sc", lambda sc: sc.activation(sg_bf[:], ps_a[:, 0:1024], AF.Silu),
                                      deps=[ev_pe_g[gi]] + ([ev_v_hmul.get(gi - 1, none_ev)] if gi >= 1 else []),
                                      inc="ac")
                hdep = [ev_a_silu[gi], ev_pe_u[gi]]
                if (Ti, e) != (0, 0):
                    prev = (Ti, e - 1) if e > 0 else (Ti - 1, 8)
                    hdep.append(ev_pe_y.get((prev[0], prev[1], 7), none_ev))
                ev_v_hmul[gi] = em.op("v", lambda v, ic=ic: v.tensor_tensor(
                    h_e[:, ic * 1024:(ic + 1) * 1024], sg_bf[:], ps_a[:, 1024:2048], ALU.mult),
                    deps=hdep, inc="dv")
            ev_pe_u_done[Ti] = ev_pe_u[gi0 + N_IC - 1]
            ev_wd_all = Em.evmax(ev_wd)

            for tc in range(8):
                yb = tc % 2
                pe_deps = [ev_v_hmul[gi0 + N_IC - 1]] + ev_wd_all
                prevkey = None
                if tc >= 2:
                    prevkey = (Ti, e, tc - 2)
                elif e > 0:
                    prevkey = (Ti, e - 1, 6 + tc)
                elif Ti > 0:
                    prevkey = (Ti - 1, 8, 6 + tc)
                if prevkey is not None:
                    pe_deps.append(ev_v_acc.get(prevkey, none_ev))
                em.wait("t", pe_deps)

                def y_mms(t, tc=tc, yb=yb):
                    last = None
                    for ic in range(N_IC):
                        for nn in range(2):
                            last = t.matmul(ps_b[:, yb * 1024 + nn * 512: yb * 1024 + (nn + 1) * 512],
                                     h_e[:, ic * 1024 + tc * 128: ic * 1024 + (tc + 1) * 128],
                                     wd_all[:, ic * 1024 + nn * 512: ic * 1024 + (nn + 1) * 512],
                                     start=(ic == 0), stop=(ic == N_IC - 1))
                    return last
                em.op("t", y_mms, inc="pe")
                ev_pe_y[(Ti, e, tc)] = ("pe", em.cnt["pe"])

                accblk = acc[:, tc * 1024:(tc + 1) * 1024]
                if e == 0:
                    adeps = [ev_pe_y[(Ti, e, tc)]]
                    if Ti > 0:
                        adeps.append(ev_v_LN2[Ti - 1][tc])
                    ev_v_acc[(Ti, e, tc)] = em.op(
                        "v", lambda v, accblk=accblk, yb=yb: v.tensor_copy(
                            accblk, ps_b[:, yb * 1024:(yb + 1) * 1024]),
                        deps=adeps, inc="dv")
                else:
                    ev_v_acc[(Ti, e, tc)] = em.op(
                        "v", lambda v, accblk=accblk, yb=yb, tc=tc, e=e: v.scalar_tensor_tensor(
                            accblk, ps_b[:, yb * 1024:(yb + 1) * 1024],
                            w_t[:, tc * NE + (e - 1): tc * NE + e], accblk,
                            ALU.mult, ALU.add),
                        deps=[ev_pe_y[(Ti, e, tc)], ev_router_w], inc="dv")

        for tc in range(8):
            ob = tc % 2
            accblk = acc[:, tc * 1024:(tc + 1) * 1024]
            deps = [ev_v_acc[(Ti, 8, tc)], ev_h1tm[tc]]
            ev_r2 = em.op("v", lambda v, accblk=accblk, tc=tc: v.tensor_add(
                r2_f[:], accblk, h1tm_t[:, tc * 1024:(tc + 1) * 1024]), deps=deps, inc="dv")
            ev_ns = em.op("v", lambda v: v.tensor_reduce(ln2[:, 0:1], r2_f[:], AX.X, ALU.add, negate=True),
                          inc="dv")
            ev_nm2 = em.op("v", lambda v: v.tensor_scalar(ln2[:, 1:2], ln2[:, 0:1], 1.0 / D, None, ALU.mult),
                           inc="dv")
            ev_rc2 = em.op("v", lambda v: v.tensor_scalar(rc2_f[:], r2_f[:], ln2[:, 1:2], None, ALU.add),
                           deps=[ev_nm2], inc="dv")
            ev_sq2 = em.op("sc", lambda sc: sc.activation(sq2[:], rc2_f[:], AF.Square,
                                                          accum_out=ln2[:, 2:3]),
                           deps=[ev_rc2], inc="ac")
            em.op("sc", lambda sc: sc.activation(ln2[:, 3:4], ln2[:, 2:3], AF.Ln,
                                                 scale=1.0 / D, bias=eps_t[:, 0:1]), inc="ac")
            ev_rs2 = em.op("sc", lambda sc: sc.activation(ln2[:, 4:5], ln2[:, 3:4], AF.Exp,
                                                          scale=-0.5), inc="ac")
            prev_out = None
            if tc >= 2:
                prev_out = ev_out_dma[Ti][tc - 2]
            elif Ti >= 1:
                prev_out = ev_out_dma[Ti - 1][tc + 6]
            ev_o = em.op("v", lambda v, ob=ob: v.tensor_scalar(out_t[ob][:], rc2_f[:], ln2[:, 4:5],
                                                               None, ALU.mult),
                         deps=[ev_rs2] + ([prev_out] if prev_out else []), inc="dv")
            ev_v_LN2[Ti][tc] = ev_o
            ev_out_dma[Ti][tc] = em.dma("sy", lambda sy, Ti=Ti, tc=tc, ob=ob: sy.dma_start(
                out=out_hbm[Ti * 1024 + tc * 128: Ti * 1024 + (tc + 1) * 128, :],
                in_=out_t[ob][:]), deps=[ev_o])

    em.wait("sy", em.fence("ds") + em.fence("dg") + em.fence("dc"))
    with nc.Block() as blkB:
        em.replay(blkB)
    phB.close()
    es.close()
    return nc


_prog_cache = {}


def _get_program():
    if "nc" not in _prog_cache:
        _prog_cache["nc"] = build_program()
    return _prog_cache["nc"]


def kernel(**inputs):
    f32 = np.float32
    x = np.asarray(inputs["L0_attn_x"], f32).reshape(32768, 1024)
    wqkv = np.ascontiguousarray(np.asarray(inputs["L0_attn_W_qkv"], f32))
    gamma = np.asarray(inputs["L0_attn_ln_gamma"], f32)
    beta = np.asarray(inputs["L0_attn_ln_beta"], f32)
    wproj = np.ascontiguousarray(np.asarray(inputs["L0_attn_W_proj"], f32))
    bproj = np.asarray(inputs["L0_attn_b_proj"], f32)
    sg = np.asarray(inputs["L0_ffn_sg_w"], f32)
    su = np.asarray(inputs["L0_ffn_su_w"], f32)
    sd = np.asarray(inputs["L0_ffn_sd_w"], f32)
    rw = np.asarray(inputs["L0_ffn_router_w"], f32)
    rb = np.asarray(inputs["L0_ffn_routing_bias"], f32)
    wg = np.asarray(inputs["L0_ffn_wg"], f32)
    wu = np.asarray(inputs["L0_ffn_wu"], f32)
    wd = np.asarray(inputs["L0_ffn_wd"], f32)

    C = np.ascontiguousarray
    gamma_t = C(np.broadcast_to(np.tile(gamma, NH), (128, D)))
    beta_t = C(np.broadcast_to(np.tile(beta, NH), (128, D)))
    bproj_t = C(np.broadcast_to(bproj, (128, D)))
    routerT = C(rw.T)
    rbias_t = C(np.broadcast_to(rb, (128, NE)))
    def slabs_IH(wm):
        return C(wm.reshape(N_IC, 128, NHC, 128).transpose(0, 3, 2, 1).reshape(N_IC, 128, D))

    def slabs_HI(wm):
        return C(wm.reshape(D, N_IC, 128).transpose(1, 2, 0))
    sgS = slabs_IH(sg)
    suS = slabs_IH(su)
    sdS = slabs_HI(sd)
    wgS = C(wg.reshape(NE, N_IC, 128, NHC, 128).transpose(0, 1, 4, 3, 2).reshape(NE, N_IC, 128, D))
    wuS = C(wu.reshape(NE, N_IC, 128, NHC, 128).transpose(0, 1, 4, 3, 2).reshape(NE, N_IC, 128, D))
    wdS = C(wd.reshape(NE, D, N_IC, 128).transpose(0, 2, 3, 1))
    ident = np.eye(128, dtype=f32)

    shared = dict(wqkv=wqkv, gamma_t=gamma_t, beta_t=beta_t, wproj=wproj,
                  bproj_t=bproj_t, routerT=routerT, rbias_t=rbias_t,
                  sgS=sgS, suS=suS, sdS=sdS, wgS=wgS, wuS=wuS, wdS=wdS,
                  ident_in=ident)
    in_maps = []
    for c in range(N_CORES):
        sl = x[c * TOK:(c + 1) * TOK]
        x_tm_c = C(sl)
        xS_c = C(sl.reshape(N_SUB, 128, NHC, 128).transpose(0, 3, 2, 1).reshape(N_SUB, 128, D))
        m = dict(shared)
        m["x_tm"] = x_tm_c
        m["xS"] = xS_c
        in_maps.append(m)

    nc = _get_program()
    res = run_bass_kernel_spmd(nc, in_maps, core_ids=list(range(N_CORES)))
    out = np.concatenate([res.results[c]["out"] for c in range(N_CORES)], axis=0)
    return out.reshape(4, 8192, 1024).astype(np.float32)


# version 08 (final)
# speedup vs baseline: 1.1168x; 1.1168x over previous
import os
import numpy as np
import ml_dtypes
from contextlib import ExitStack

import concourse.bass as bass
import concourse.mybir as mybir
from concourse.bass_utils import run_bass_kernel_spmd

F32 = mybir.dt.float32
BF16 = mybir.dt.bfloat16
AF = mybir.ActivationFunctionType
ALU = mybir.AluOpType
AX = mybir.AxisListType

N_CORES = 8
TOK = 4096
N_SUB = 32
N_TILE = 4
SUB_PER_TILE = 8
D = 1024
NHC = 8
QKV = 3072
I_DIM = 2048
N_IC = 16
NE = 8
W = 64
NH = 16
HD = 64
SCALE = 0.125
EPS = 1e-5
_STAGE = int(os.environ.get("KSTAGE", "9999"))


class Em:

    ENGS = ("t", "v", "sc", "g", "sy")

    QLANES = {"ds": 4, "dg": 8, "dc": 3}

    def __init__(self, nc, es):
        self.nc = nc
        self.q = {e: [] for e in self.ENGS}
        self.sems = {}
        self.cnt = {}
        for s in ("pe", "dv", "ac", "po"):
            self.sems[s] = es.enter_context(nc.semaphore("sem_" + s))
            self.cnt[s] = 0
        self.rot = {}
        for qn, nl in self.QLANES.items():
            self.rot[qn] = 0
            for i in range(nl):
                s = f"{qn}{i}"
                self.sems[s] = es.enter_context(nc.semaphore("sem_" + s))
                self.cnt[s] = 0

    def dma(self, eng, fn, deps=()):
        qn = {"sy": "ds", "g": "dg", "sc": "dc"}[eng]
        i = self.rot[qn]
        self.rot[qn] = (i + 1) % self.QLANES[qn]
        lane = f"{qn}{i}"
        deps2 = [(lane, self.cnt[lane])] + list(deps)
        return self.op(eng, fn, deps=deps2, inc=lane, inck=16)

    @staticmethod
    def evmax(evs):
        d = {}
        for (s, c) in evs:
            d[s] = max(d.get(s, 0), c)
        return list(d.items())

    def fence(self, qn):
        return [(f"{qn}{i}", self.cnt[f"{qn}{i}"]) for i in range(self.QLANES[qn])]

    ENG_SEM = {"v": "dv", "sc": "ac", "g": "po"}

    def op(self, eng, fn, deps=(), inc=None, inck=1):
        es = self.ENG_SEM.get(eng)
        if es is not None and inc != es and not str(inc or "").startswith(("ds", "dg", "dc")):
            deps = list(deps) + [(es, self.cnt[es])]
            if inc is None:
                inc = es
        elif es is not None and inc == es:
            deps = list(deps) + [(es, self.cnt[es])]
        deps = [(s, c) for (s, c) in deps if c > 0]
        ev = None
        if inc is not None:
            self.cnt[inc] += inck
            ev = (inc, self.cnt[inc])
        sems = self.sems

        def run(e, fn=fn, deps=deps, inc=inc, inck=inck):
            for (s, c) in deps:
                e.wait_ge(sems[s], c)
            inst = fn(e)
            if inc is not None:
                inst.then_inc(sems[inc], inck)
        self.q[eng].append(run)
        return ev

    def wait(self, eng, deps):
        deps = [(s, c) for (s, c) in deps if c > 0]
        if not deps:
            return
        sems = self.sems

        def run(e, deps=deps):
            for (s, c) in deps:
                e.wait_ge(sems[s], c)
        self.q[eng].append(run)

    def replay(self, blk):
        def mk(lst):
            def body(eng):
                for f in lst:
                    f(eng)
            return body
        blk.tensor(mk(self.q["t"]))
        blk.vector(mk(self.q["v"]))
        blk.scalar(mk(self.q["sc"]))
        blk.gpsimd(mk(self.q["g"]))
        blk.sync(mk(self.q["sy"]))
        self.q = {e: [] for e in self.ENGS}


def build_program(n_sub=N_SUB, n_tile=N_TILE):
    nc = bass.Bass("TRN2", target_bir_lowering=False,
                   detect_race_conditions=not int(os.environ.get("KNORACE", "0")))
    dr = lambda n, sh, dt, k: nc.dram_tensor(n, sh, dt, kind=k).ap()
    xS = dr("xS", [N_SUB, 128, D], F32, "ExternalInput")
    x_tm = dr("x_tm", [TOK, D], F32, "ExternalInput")
    wqkv = dr("wqkv", [D, QKV], F32, "ExternalInput")
    gamma_t = dr("gamma_t", [128, D], F32, "ExternalInput")
    beta_t = dr("beta_t", [128, D], F32, "ExternalInput")
    wproj = dr("wproj", [D, D], F32, "ExternalInput")
    bproj_t = dr("bproj_t", [128, D], F32, "ExternalInput")
    routerT = dr("routerT", [D, NE], F32, "ExternalInput")
    rbias_t = dr("rbias_t", [128, NE], F32, "ExternalInput")
    sgS = dr("sgS", [N_IC, 128, D], F32, "ExternalInput")
    suS = dr("suS", [N_IC, 128, D], F32, "ExternalInput")
    sdS = dr("sdS", [N_IC, 128, D], F32, "ExternalInput")
    wgS = dr("wgS", [NE, N_IC, 128, D], F32, "ExternalInput")
    wuS = dr("wuS", [NE, N_IC, 128, D], F32, "ExternalInput")
    wdS = dr("wdS", [NE, N_IC, 128, D], F32, "ExternalInput")
    ident_in = dr("ident_in", [128, 128], F32, "ExternalInput")
    out_hbm = dr("out", [TOK, D], F32, "ExternalOutput")
    h1fm_s = dr("h1fm_s", [NHC, N_SUB, 128, 128], BF16, "Internal")
    h1tm_s = dr("h1tm_s", [TOK, D], F32, "Internal")

    es = ExitStack()
    T = lambda n, sh, dt: es.enter_context(nc.sbuf_tensor(n, sh, dt))
    em = Em(nc, es)

    ident = T("ident", [128, 128], F32)
    ident_bf = T("ident_bf", [128, 128], BF16)
    eps_t = T("eps_t", [128, 1], F32)
    router_bf = T("router_bf", [128, NHC * NE], BF16)
    rbias = T("rbias", [128, NE], F32)
    ps_a = es.enter_context(nc.psum_tensor("ps_a", [128, 2048], F32))
    ps_b = es.enter_context(nc.psum_tensor("ps_b", [128, 2048], F32))
    ps_b_hi_bf = ps_b[:, 1024:2048].bitcast(BF16)
    ps_b_lo_bf = ps_b[:, 0:1024].bitcast(BF16)

    phA = ExitStack()
    TA = lambda n, sh, dt: phA.enter_context(nc.sbuf_tensor(n, sh, dt))
    wqkv_bf = TA("wqkv_bf", [128, NHC * QKV], BF16)
    wproj_bf = TA("wproj_bf", [128, NHC * D], BF16)
    gamma_qk = TA("gamma_qk", [128, 2048], F32)
    beta_qk = TA("beta_qk", [128, 2048], F32)
    bproj = TA("bproj", [128, D], F32)
    gamma_sb = TA("gamma_sb", [128, D], F32)
    beta_sb = TA("beta_sb", [128, D], F32)
    xfm_bf = [TA(f"xfm_bf{i}", [128, D], BF16) for i in range(2)]
    xtm_f = [TA(f"xtm_f{i}", [128, D], F32) for i in range(2)]
    qn0 = TA("qn0", [128, 2048], F32)
    sq_sb = TA("sq_sb", [128, 2048], F32)
    qng = TA("qng", [128, 2048], F32)
    qnk_bf = TA("qnk_bf", [128, 2048], BF16)
    v_bf = TA("v_bf", [128, D], BF16)
    v_x = TA("v_x", [128, D], BF16)
    qnT_bf = TA("qnT_bf", [128, D], BF16)
    knT_bf = TA("knT_bf", [128, D], BF16)
    exp_in = TA("exp_in", [64, 2048], F32)
    expv = TA("expv", [64, 2048], F32)
    S_n = TA("S_n", [64, 2048], F32)
    ST_bf = TA("ST_bf", [128, D], BF16)
    o_bf = TA("o_bf", [128, D], BF16)
    r_f = TA("r_f", [128, D], F32)
    rc_f = TA("rc_f", [128, D], F32)
    h1_tm = TA("h1_tm", [128, D], F32)
    h1_bf = TA("h1_bf", [128, D], BF16)
    h1T_bf = TA("h1T_bf", [128, D], BF16)
    stat = TA("stat", [128, 32], F32)
    stat2 = TA("stat2", [128, 32], F32)
    negmu = TA("negmu", [128, 32], F32)
    mu2 = TA("mu2", [128, 32], F32)
    var_t = TA("var_t", [128, 32], F32)
    rs_t = TA("rs_t", [128, 32], F32)
    negmax = TA("negmax", [64, 32], F32)
    sums = TA("sums", [64, 32], F32)
    rsum = TA("rsum", [64, 32], F32)
    ln1 = TA("ln1", [128, 8], F32)


    ev_wqkv = em.dma("g", lambda g: g.dma_start(
        out=wqkv_bf[:].rearrange("p (c n) -> p c n", c=NHC),
        in_=wqkv.rearrange("(c p) n -> p c n", p=128)))
    ev_wproj = em.dma("g", lambda g: g.dma_start(
        out=wproj_bf[:].rearrange("p (c n) -> p c n", c=NHC),
        in_=wproj.rearrange("(c p) n -> p c n", p=128)))
    ev_router = em.dma("g", lambda g: g.dma_start(
        out=router_bf[:].rearrange("p (c e) -> p c e", c=NHC),
        in_=routerT.rearrange("(c p) e -> p c e", p=128)))
    ev_ident = em.dma("sy", lambda sy: sy.dma_start(out=ident[:], in_=ident_in))
    ev_identbf = em.dma("g", lambda g: g.dma_start(out=ident_bf[:], in_=ident_in))
    ev_gam = em.dma("sy", lambda sy: sy.dma_start(out=gamma_sb[:], in_=gamma_t))
    ev_bet = em.dma("sy", lambda sy: sy.dma_start(out=beta_sb[:], in_=beta_t))
    ev_bpj = em.dma("sy", lambda sy: sy.dma_start(out=bproj[:], in_=bproj_t))
    ev_rbias = em.dma("sy", lambda sy: sy.dma_start(out=rbias[:], in_=rbias_t))
    ev_eps = em.op("g", lambda g: g.memset(eps_t[:], EPS), inc="po")
    ev_cq = em.op("v", lambda v: v.tensor_scalar(gamma_qk[:, 0:1024], gamma_sb[:], SCALE, None, ALU.mult),
                  deps=[ev_gam], inc="dv")
    em.op("v", lambda v: v.tensor_copy(gamma_qk[:, 1024:2048], gamma_sb[:]), inc="dv")
    em.op("v", lambda v: v.tensor_scalar(beta_qk[:, 0:1024], beta_sb[:], SCALE, None, ALU.mult),
          deps=[ev_bet], inc="dv")
    ev_consts = em.op("v", lambda v: v.tensor_copy(beta_qk[:, 1024:2048], beta_sb[:]), inc="dv")

    Z = (None, 0)
    none_ev = ("dv", 0)
    ev_pe_qkv = [none_ev] * N_SUB
    ev_pe_qnT = [none_ev] * N_SUB
    ev_pe_scores = [none_ev] * N_SUB
    ev_pe_ST = [none_ev] * N_SUB
    ev_pe_attnv = [none_ev] * N_SUB
    ev_pe_proj = [none_ev] * N_SUB
    ev_pe_h1T = [none_ev] * N_SUB
    ev_v_center = [none_ev] * N_SUB
    ev_v_ssum = [none_ev] * N_SUB
    ev_v_vevac = [none_ev] * N_SUB
    ev_v_qnTevac = [none_ev] * N_SUB
    ev_v_expin = [none_ev] * N_SUB
    ev_v_Sn = [none_ev] * N_SUB
    ev_v_STevac = [none_ev] * N_SUB
    ev_v_oevac = [none_ev] * N_SUB
    ev_v_prE = [none_ev] * N_SUB
    ev_v_resid = [none_ev] * N_SUB
    ev_v_h1 = [none_ev] * N_SUB
    ev_v_h1bf = [none_ev] * N_SUB
    ev_v_h1Tevac = [none_ev] * N_SUB
    ev_a_sq = [none_ev] * N_SUB
    ev_a_rs = [none_ev] * N_SUB
    ev_a_exp = [none_ev] * N_SUB
    ev_a_sq1 = [none_ev] * N_SUB
    ev_a_rs1 = [none_ev] * N_SUB
    ev_g_qnk = [none_ev] * N_SUB
    ev_ds_vx = [none_ev] * N_SUB
    ev_ds_vx1 = [none_ev] * N_SUB
    ev_ds_xtm = [none_ev] * N_SUB
    ev_dg_xfm = [none_ev] * N_SUB
    ev_dc_h1T = [none_ev] * N_SUB
    ev_dc_h1tm = [none_ev] * N_SUB

    for st in range(n_sub):
        b = st % 2
        P = lambda i: (st - i) if st >= i else None

        deps = [ev_pe_qkv[st - 2], ev_dg_xfm[st - 2]] if st >= 2 else []
        ev_dg_xfm[st] = em.dma("g", lambda g, st=st, b=b: g.dma_start(
            out=xfm_bf[b][:], in_=xS[st]), deps=deps)
        deps = [ev_v_resid[st - 2], ev_ds_xtm[st - 2]] if st >= 2 else []
        ev_ds_xtm[st] = em.dma("sy", lambda sy, st=st, b=b: sy.dma_start(
            out=xtm_f[b][:], in_=x_tm[st * 128:(st + 1) * 128, :]), deps=deps)

        pe_deps = [ev_wqkv, ev_dg_xfm[st]]
        if st >= 1:
            pe_deps += [ev_v_expin[st - 1]]
        em.wait("t", pe_deps)

        def qk_mms(t, b=b):
            last = None
            for hc in range(NHC):
                for nn in range(4):
                    last = t.matmul(ps_a[:, nn * 512:(nn + 1) * 512],
                             xfm_bf[b][:, hc * 128:(hc + 1) * 128],
                             wqkv_bf[:, hc * QKV + nn * 512: hc * QKV + (nn + 1) * 512],
                             start=(hc == 0), stop=(hc == NHC - 1))
            return last
        em.op("t", qk_mms, inc="pe")
        ev_pe_qk = ("pe", em.cnt["pe"])

        def v_mms(t, b=b):
            last = None
            for hc in range(NHC):
                for nn in range(2):
                    last = t.matmul(ps_b[:, 1024 + nn * 512:1024 + (nn + 1) * 512],
                             xfm_bf[b][:, hc * 128:(hc + 1) * 128],
                             wqkv_bf[:, hc * QKV + 2048 + nn * 512: hc * QKV + 2048 + (nn + 1) * 512],
                             start=(hc == 0), stop=(hc == NHC - 1))
            return last
        if st >= 1:
            em.wait("t", [ev_v_prE[st - 1]])
        em.op("t", v_mms, inc="pe")
        ev_pe_qkv[st] = ("pe", em.cnt["pe"])

        if _STAGE <= 1:
            continue
        ps_qk3 = ps_a[:].rearrange("p (s k) -> p s k", s=32)
        dv_deps = [ev_pe_qk]
        if st >= 1:
            dv_deps.append(ev_a_sq[st - 1])
        ev_st1 = em.op("v", lambda v: v.tensor_reduce(
            stat[:].unsqueeze(2), ps_qk3, AX.X, ALU.add, negate=True),
            deps=dv_deps, inc="dv")
        if _STAGE <= 2:
            continue
        ev_a_sq[st] = em.op("sc", lambda sc: sc.activation(sq_sb[:], ps_a[:], AF.Square),
                            deps=[ev_pe_qk, ev_st1] + ([ev_v_ssum[st - 1]] if st >= 1 else []),
                            inc="ac")
        ev_v_ssum[st] = em.op("v", lambda v: v.tensor_reduce(
            stat2[:].unsqueeze(2),
            sq_sb[:].rearrange("p (s k) -> p s k", s=32), AX.X, ALU.add),
            deps=[ev_a_sq[st]], inc="dv")
        if _STAGE <= 3:
            continue
        em.op("v", lambda v: v.tensor_scalar(negmu[:], stat[:], 1.0 / 64, None, ALU.mult),
              deps=([ev_a_rs[st - 1]] if st >= 1 else []), inc="dv")
        em.op("v", lambda v: v.tensor_mul(mu2[:], negmu[:], negmu[:]), inc="dv")
        em.op("v", lambda v: v.tensor_scalar(var_t[:], stat2[:], 1.0 / 64, None, ALU.mult), inc="dv")
        ev_var = em.op("v", lambda v: v.tensor_sub(var_t[:], var_t[:], mu2[:]), inc="dv")
        if _STAGE <= 4:
            continue
        ev_lnv = em.op("sc", lambda sc: sc.activation(var_t[:], var_t[:], AF.Ln, bias=eps_t[:, 0:1]),
                       deps=[ev_var, ev_eps], inc="ac")
        ev_a_rs[st] = em.op("sc", lambda sc: sc.activation(rs_t[:], var_t[:], AF.Exp, scale=-0.5),
                            deps=([ev_v_center[st - 1]] if st >= 1 else []), inc="ac")
        if _STAGE <= 5:
            continue
        nmu_bc = negmu[:].unsqueeze(2).broadcast_to((128, 32, 64))
        rs_bc = rs_t[:].unsqueeze(2).broadcast_to((128, 32, 64))
        q3 = qn0[:].rearrange("p (s k) -> p s k", s=32)
        em.op("v", lambda v: v.tensor_tensor(q3, ps_qk3, nmu_bc, ALU.add),
              deps=[ev_st1, ev_a_rs[st]] + ([ev_g_qnk[st - 1]] if st >= 1 else []), inc="dv")
        ev_v_center[st] = em.op("v", lambda v: v.tensor_tensor(q3, q3, rs_bc, ALU.mult), inc="dv")
        if _STAGE <= 6:
            continue
        dep_prev = [ev_pe_qnT[st - 1]] if st >= 1 else []
        em.op("g", lambda g: g.tensor_mul(qng[:, 0:1024], qn0[:, 0:1024], gamma_qk[:, 0:1024]),
              deps=[ev_v_center[st], ev_consts] + dep_prev, inc="po")
        ev_g_q = em.op("g", lambda g: g.tensor_add(qnk_bf[:, 0:1024], qng[:, 0:1024], beta_qk[:, 0:1024]), inc="po")
        em.op("v", lambda v: v.tensor_mul(qng[:, 1024:2048], qn0[:, 1024:2048], gamma_qk[:, 1024:2048]),
              deps=[ev_consts] + dep_prev, inc="dv")
        ev_v_k = em.op("v", lambda v: v.tensor_add(qnk_bf[:, 1024:2048], qng[:, 1024:2048], beta_qk[:, 1024:2048]), inc="dv")
        ev_g_qnk[st] = ev_g_q
        ev_v_qnk = ev_v_k

        if _STAGE <= 7:
            continue
        ev_v_vevac[st] = em.op("v", lambda v: v.tensor_copy(v_bf[:], ps_b[:, 1024:2048]),
                               deps=[ev_pe_qkv[st]] + ([ev_ds_vx[st - 1], ev_ds_vx1[st - 1]] if st >= 1 else []),
                               inc="dv")
        ev_vx1 = em.dma("sy", lambda sy: sy.dma_start(out=v_x[0:64, :], in_=v_bf[64:128, :]),
                        deps=[ev_v_vevac[st]] + ([ev_pe_attnv[st - 1], ev_ds_vx1[st - 1]] if st >= 1 else []))
        ev_vx2 = em.dma("sy", lambda sy: sy.dma_start(out=v_x[64:128, :], in_=v_bf[0:64, :]),
                        deps=([ev_ds_vx[st - 1]] if st >= 1 else []))
        ev_ds_vx[st] = ev_vx2
        ev_ds_vx1[st] = ev_vx1

        if _STAGE <= 8:
            continue
        t_deps = [ev_g_qnk[st], ev_v_qnk, ev_ident, ev_identbf, ev_v_vevac[st]]
        em.wait("t", t_deps)

        def qnT_mms(t):
            last = None
            for c in range(2 * NHC):
                last = t.transpose(ps_b_hi_bf[:, c * 128:(c + 1) * 128],
                            qnk_bf[:, c * 128:(c + 1) * 128],
                            ident_bf[:])
            return last
        em.op("t", qnT_mms, inc="pe")
        ev_pe_qnT[st] = ("pe", em.cnt["pe"])

        em.op("v", lambda v: v.tensor_copy(qnT_bf[:], ps_b_hi_bf[:, 0:1024]),
              deps=[ev_pe_qnT[st]] + ([ev_pe_scores[st - 1]] if st >= 1 else []), inc="dv")
        ev_v_qnTevac[st] = em.op("v", lambda v: v.tensor_copy(knT_bf[:], ps_b_hi_bf[:, 1024:2048]),
                                 inc="dv")

        if _STAGE <= 9:
            continue
        em.wait("t", [ev_v_qnTevac[st], ev_v_center[st], ev_a_sq[st]])

        def score_mms_par(par):
            def f(t):
                last = None
                for w in range(2):
                    for h in range(par, NH, 2):
                        c, hp = h // 2, (h % 2) * 64
                        s = w * NH + h
                        last = t.matmul(ps_a[0:64, s * 64:(s + 1) * 64],
                                 qnT_bf[hp:hp + 64, c * 128 + w * 64: c * 128 + w * 64 + 64],
                                 knT_bf[hp:hp + 64, c * 128 + w * 64: c * 128 + w * 64 + 64],
                                 start=True, stop=True, tile_position=(hp, 0))
                return last
            return f
        em.op("t", score_mms_par(0), inc="pe")
        em.wait("t", [("pe", em.cnt["pe"])])
        em.op("t", score_mms_par(1), inc="pe")
        ev_pe_scores[st] = ("pe", em.cnt["pe"])

        if _STAGE <= 10:
            continue
        sc3 = ps_a[0:64, :].rearrange("p (s k) -> p s k", s=32)
        ev_nm = em.op("v", lambda v: v.tensor_reduce(
            negmax[:].unsqueeze(2), sc3, AX.X, ALU.max, negate=True),
            deps=[ev_pe_scores[st]], inc="dv")
        nmax_bc = negmax[:].unsqueeze(2).broadcast_to((64, 32, 64))
        ev_v_expin[st] = em.op("v", lambda v: v.tensor_tensor(
            exp_in[:].rearrange("p (s k) -> p s k", s=32), sc3, nmax_bc, ALU.add),
            deps=[ev_nm] + ([ev_a_exp[st - 1]] if st >= 1 else []), inc="dv")
        ev_a_exp[st] = em.op("sc", lambda sc: sc.activation(expv[:], exp_in[:], AF.Exp),
                             deps=[ev_v_expin[st]] + ([ev_v_Sn[st - 1]] if st >= 1 else []),
                             inc="ac")
        em.op("v", lambda v: v.tensor_reduce(
            sums[:].unsqueeze(2),
            expv[:].rearrange("p (s k) -> p s k", s=32), AX.X, ALU.add),
            deps=[ev_a_exp[st]], inc="dv")
        em.op("v", lambda v: v.reciprocal(rsum[:], sums[:]), inc="dv")
        rsum_bc = rsum[:].unsqueeze(2).broadcast_to((64, 32, 64))
        ev_v_Sn[st] = em.op("v", lambda v: v.tensor_tensor(
            S_n[:].rearrange("p (s k) -> p s k", s=32),
            expv[:].rearrange("p (s k) -> p s k", s=32), rsum_bc, ALU.mult),
            deps=[] if st == 0 else [ev_pe_ST[st - 1]], inc="dv")

        if _STAGE <= 11:
            continue
        em.wait("t", [ev_v_Sn[st]] + ([ev_v_h1Tevac[st - 1]] if st >= 1 else []))

        def st_mms(t):
            last = None
            for j in range(16):
                last = t.transpose(ps_b[:, j * 64:(j + 1) * 64],
                            S_n[0:64, j * 128:(j + 1) * 128],
                            ident[0:64, 0:64])
            return last
        em.op("t", st_mms, inc="pe")
        ev_pe_ST[st] = ("pe", em.cnt["pe"])

        ev_v_STevac[st] = em.op("v", lambda v: v.tensor_copy(ST_bf[:], ps_b[:, 0:1024]),
                                deps=[ev_pe_ST[st]] + ([ev_pe_attnv[st - 1]] if st >= 1 else []),
                                inc="dv")

        if _STAGE <= 12:
            continue
        em.wait("t", [ev_v_STevac[st], ev_ds_vx1[st], ev_ds_vx[st]])

        def av_mms_par(par):
            def f(t):
                last = None
                for w in range(2):
                    for h in range(par, NH, 2):
                        c, hp = h // 2, (h % 2) * 64
                        j = w * 8 + h // 2
                        vsrc = v_bf if (w == hp // 64) else v_x
                        last = t.matmul(ps_b[hp:hp + 64, c * 128 + w * 64: c * 128 + w * 64 + 64],
                                 vsrc[hp:hp + 64, h * 64:(h + 1) * 64],
                                 ST_bf[hp:hp + 64, j * 64:(j + 1) * 64],
                                 start=True, stop=True, tile_position=(hp, hp))
                return last
            return f
        em.op("t", av_mms_par(0), inc="pe")
        em.wait("t", [("pe", em.cnt["pe"])])
        em.op("t", av_mms_par(1), inc="pe")
        ev_pe_attnv[st] = ("pe", em.cnt["pe"])

        ev_v_oevac[st] = em.op("v", lambda v: v.tensor_copy(o_bf[:], ps_b[:, 0:1024]),
                               deps=[ev_pe_attnv[st]] + ([ev_pe_proj[st - 1]] if st >= 1 else []),
                               inc="dv")

        if _STAGE <= 13:
            continue
        em.wait("t", [ev_v_oevac[st], ev_wproj, ev_v_qnTevac[st]])

        def proj_mms(t):
            last = None
            for c in range(NHC):
                for nn in range(2):
                    last = t.matmul(ps_b[:, 1024 + nn * 512:1024 + (nn + 1) * 512],
                             o_bf[:, c * 128:(c + 1) * 128],
                             wproj_bf[:, c * D + nn * 512: c * D + (nn + 1) * 512],
                             start=(c == 0), stop=(c == NHC - 1))
            return last
        em.op("t", proj_mms, inc="pe")
        ev_pe_proj[st] = ("pe", em.cnt["pe"])

        if _STAGE <= 14:
            continue
        ev_r0 = em.op("v", lambda v: v.scalar_tensor_tensor(
            r_f[:], ps_b[:, 1024:2048], 1.0, bproj[:], ALU.mult, ALU.add),
            deps=[ev_pe_proj[st], ev_bpj] + ([ev_a_sq1[st - 1]] if st >= 1 else []), inc="dv")
        ev_v_resid[st] = em.op("v", lambda v, b=b: v.tensor_add(r_f[:], r_f[:], xtm_f[b][:]),
                               deps=[ev_ds_xtm[st]], inc="dv")
        ev_prE_ = em.op("v", lambda v: v.tensor_reduce(ln1[:, 0:1], r_f[:], AX.X, ALU.add, negate=True),
                        inc="dv")
        ev_v_prE[st] = ev_r0
        ev_nm1 = em.op("v", lambda v: v.tensor_scalar(ln1[:, 1:2], ln1[:, 0:1], 1.0 / D, None, ALU.mult),
                       inc="dv")
        ev_rc = em.op("v", lambda v: v.tensor_scalar(rc_f[:], r_f[:], ln1[:, 1:2], None, ALU.add),
                      deps=[ev_nm1], inc="dv")
        ev_a_sq1[st] = em.op("sc", lambda sc: sc.activation(sq_sb[:, 0:1024], rc_f[:], AF.Square,
                                                            accum_out=ln1[:, 2:3]),
                             deps=[ev_rc, ev_v_ssum[st]], inc="ac")
        em.op("sc", lambda sc: sc.activation(ln1[:, 3:4], ln1[:, 2:3], AF.Ln,
                                             scale=1.0 / D, bias=eps_t[:, 0:1]), inc="ac")
        ev_a_rs1[st] = em.op("sc", lambda sc: sc.activation(ln1[:, 4:5], ln1[:, 3:4], AF.Exp,
                                                            scale=-0.5),
                             deps=([ev_v_h1[st - 1]] if st >= 1 else []), inc="ac")
        ev_v_h1[st] = em.op("v", lambda v: v.tensor_scalar(h1_tm[:], rc_f[:], ln1[:, 4:5], None, ALU.mult),
                            deps=[ev_a_rs1[st]] + ([ev_dc_h1tm[st - 1]] if st >= 1 else []),
                            inc="dv")
        ev_v_h1bf[st] = em.op("v", lambda v: v.tensor_copy(h1_bf[:], h1_tm[:]),
                              deps=([ev_pe_h1T[st - 1]] if st >= 1 else []), inc="dv")

        if _STAGE <= 15:
            continue
        em.wait("t", [ev_v_h1bf[st], ev_v_oevac[st]])

        def h1T_mms(t):
            last = None
            for c in range(NHC):
                last = t.transpose(ps_b_lo_bf[:, c * 128:(c + 1) * 128],
                            h1_bf[:, c * 128:(c + 1) * 128],
                            ident_bf[:])
            return last
        em.op("t", h1T_mms, inc="pe")
        ev_pe_h1T[st] = ("pe", em.cnt["pe"])

        ev_v_h1Tevac[st] = em.op("v", lambda v: v.tensor_copy(h1T_bf[:], ps_b_lo_bf[:, 0:1024]),
                                 deps=[ev_pe_h1T[st]] + ([ev_dc_h1T[st - 1]] if st >= 1 else []),
                                 inc="dv")

        if _STAGE <= 16:
            continue
        _KSTORE = int(os.environ.get("KSTORE", "3"))
        _h1q = "sy" if int(os.environ.get("KH1SY", "0")) else "sc"
        if _KSTORE & 2:
            ev_dc_h1T[st] = em.dma(_h1q, lambda sc, st=st: sc.dma_start(
                out=h1fm_s[:, st].rearrange("c p t -> p c t"),
                in_=h1T_bf[:].rearrange("p (c t) -> p c t", c=NHC)),
                deps=[ev_v_h1Tevac[st]])
        if _KSTORE & 1:
            ev_dc_h1tm[st] = em.dma(_h1q, lambda sc, st=st: sc.dma_start(
                out=h1tm_s[st * 128:(st + 1) * 128, :], in_=h1_tm[:]),
                deps=[ev_v_h1[st]])

    endA = ([(s, em.cnt[s]) for s in ("pe", "dv", "ac", "po")]
            + em.fence("ds") + em.fence("dg") + em.fence("dc"))
    fence_dc_A = em.fence("dc")

    with nc.Block() as blkA:
        em.replay(blkA)
    phA.close()

    phB = ExitStack()
    TB = lambda n, sh, dt: phB.enter_context(nc.sbuf_tensor(n, sh, dt))
    h1fm_t = [TB(f"h1fm_t{i}", [128, NHC * 1024], BF16) for i in range(2)]
    h1tm_t = TB("h1tm_t", [128, 8 * 1024], F32)
    acc = TB("acc", [128, 8 * 1024], F32)
    h_e = TB("h_e", [128, N_IC * 1024], BF16)
    wd_all = TB("wd_all", [128, N_IC * 1024], BF16)
    wg_ring = [TB(f"wg_ring{i}", [128, 1024], BF16) for i in range(3)]
    wu_ring = [TB(f"wu_ring{i}", [128, 1024], BF16) for i in range(3)]
    sg_bf = TB("sg_bf", [128, 1024], BF16)
    lg_t = TB("lg_t", [128, 8 * NE], F32)
    srt_t = TB("srt_t", [128, NE], F32)
    e12 = TB("e12", [128, 2], F32)
    p12 = TB("p12", [128, 2], F32)
    pn = TB("pn", [128, 2], F32)
    wn_t = TB("wn_t", [128, 2], F32)
    m1_t = TB("m1_t", [128, NE], F32)
    m2_t = TB("m2_t", [128, NE], F32)
    mt_t = TB("mt_t", [128, NE], F32)
    w_t = TB("w_t", [128, 8 * NE], F32)
    r2_f = TB("r2_f", [128, 1024], F32)
    rc2_f = TB("rc2_f", [128, 1024], F32)
    sq2 = TB("sq2", [128, 1024], F32)
    ln2 = TB("ln2", [128, 8], F32)
    out_t = [TB(f"out_t{i}", [128, 1024], F32) for i in range(2)]

    em.wait("t", endA)
    em.wait("v", endA)
    em.wait("sc", endA)
    em.wait("g", endA)
    em.wait("sy", endA)

    ev_v_LN2 = [[none_ev] * 8 for _ in range(N_TILE)]
    ev_out_dma = [[none_ev] * 8 for _ in range(N_TILE)]
    ev_v_acc = {}
    ev_pe_g = {}
    ev_pe_u = {}
    ev_a_silu = {}
    ev_v_hmul = {}
    ev_pe_y = {}
    ev_dg_slab = {}
    ev_pe_router_last = none_ev
    ev_pe_u_done = [none_ev] * N_TILE
    ev_pe_y_done = [none_ev] * N_TILE
    ev_h1fm_prev = [none_ev] * NHC
    ev_h1fm_tile = [None] * (N_TILE + 1)
    ev_router_w_tile = [None] * (N_TILE + 1)
    ev_h1tm_prev = [none_ev] * 8
    ev_dg_gslab = {}
    ev_dg_uslab = {}
    ev_wd_prev = [none_ev] * N_IC

    _KPB = int(os.environ.get("KPB", "0"))
    for Ti in range(n_tile):
        def emit_h1fm_loads(Tl):
            buf = h1fm_t[Tl % 2]
            ld = [endA[0]] if Tl == 0 else [ev_pe_u_done[Tl - 2]] if Tl >= 2 else []
            if Tl == 0:
                ld = ld + fence_dc_A
            evs = []
            for c in range(NHC):
                evs.append(em.dma("sy", lambda sy, Tl=Tl, c=c, buf=buf: sy.dma_start(
                    out=buf[:, c * 1024:(c + 1) * 1024].rearrange("p (s t) -> p s t", s=8),
                    in_=h1fm_s[c, Tl * 8:(Tl + 1) * 8].rearrange("s p t -> p s t")),
                    deps=(ld if c == 0 else []) + [ev_h1fm_prev[c]]))
                ev_h1fm_prev[c] = evs[-1]
            ev_h1fm_tile[Tl] = Em.evmax(evs)
        if Ti == 0:
            emit_h1fm_loads(0)
        ev_h1fm_all = ev_h1fm_tile[Ti]
        h1fm_cur = h1fm_t[Ti % 2]
        ev_h1tm = []
        for tc in range(8):
            if _KPB & 2 and Ti >= 1:
                ev_h1tm.append(none_ev)
                continue
            dps = [] if Ti == 0 else [ev_v_LN2[Ti - 1][tc]]
            if Ti == 0 and tc == 0:
                dps = fence_dc_A
            ev_h1tm.append(em.dma("sy", lambda sy, Ti=Ti, tc=tc: sy.dma_start(
                out=h1tm_t[:, tc * 1024:(tc + 1) * 1024],
                in_=h1tm_s[Ti * 1024 + tc * 128: Ti * 1024 + (tc + 1) * 128, :]),
                deps=dps + [ev_h1tm_prev[tc]]))
            ev_h1tm_prev[tc] = ev_h1tm[-1]

        def emit_router(Tr):
            ev_v_w = none_ev
            ev_lgread = none_ev
            for tc in range(8):
                rt_deps = ev_h1fm_tile[Tr] + [ev_router, ev_lgread]
                if Tr > 0 and tc == 0:
                    rt_deps.append(ev_v_acc.get((Tr - 1, 8, 6), none_ev))
                em.wait("t", rt_deps)
    
                def rt_mms(t, Tr=Tr, tc=tc, hb=h1fm_t[Tr % 2]):
                    last = None
                    for c in range(NHC):
                        last = t.matmul(ps_b[:, 0:NE],
                                 hb[:, c * 1024 + tc * 128: c * 1024 + (tc + 1) * 128],
                                 router_bf[:, c * NE:(c + 1) * NE],
                                 start=(c == 0), stop=(c == NHC - 1))
                    return last
                em.op("t", rt_mms, inc="pe")
                ev_rt = ("pe", em.cnt["pe"])
                lg = lg_t[:, tc * NE:(tc + 1) * NE]
                wcol = w_t[:, tc * NE:(tc + 1) * NE]
                ev1 = em.op("v", lambda v, lg=lg: v.scalar_tensor_tensor(
                    lg, ps_b[:, 0:NE], 1.0, rbias[:], ALU.mult, ALU.add),
                    deps=[ev_rt, ev_rbias], inc="dv")
                ev_lgread = ev1
                ev2 = em.op("v", lambda v, lg=lg: v.max(srt_t[:], lg), deps=[ev1], inc="dv")
                ev3 = em.op("sc", lambda sc: sc.activation(e12[:], srt_t[:, 0:2], AF.Exp, scale=-1.0),
                            deps=[ev2], inc="ac")
                ev4 = em.op("v", lambda v: v.tensor_scalar(p12[:], e12[:], 1.0, None, ALU.add),
                            deps=[ev3], inc="dv")
                ev5 = em.op("v", lambda v: v.reciprocal(p12[:], p12[:]), inc="dv")
                ev6 = em.op("v", lambda v: v.tensor_add(pn[:, 0:1], p12[:, 0:1], p12[:, 1:2]), inc="dv")
                ev7 = em.op("v", lambda v: v.reciprocal(pn[:, 1:2], pn[:, 0:1]), deps=[ev6], inc="dv")
                ev8 = em.op("v", lambda v: v.tensor_scalar(wn_t[:], p12[:], pn[:, 1:2], None, ALU.mult),
                            deps=[ev7], inc="dv")
                ev9 = em.op("v", lambda v, lg=lg: v.tensor_scalar(m1_t[:], lg, srt_t[:, 0:1], None, ALU.is_equal),
                            deps=[ev2], inc="dv")
                em.op("v", lambda v, lg=lg: v.tensor_scalar(m2_t[:], lg, srt_t[:, 1:2], None, ALU.is_equal),
                      inc="dv")
                em.op("v", lambda v: v.tensor_mul(mt_t[:], m1_t[:], m2_t[:]), inc="dv")
                em.op("v", lambda v: v.tensor_sub(m2_t[:], m2_t[:], mt_t[:]), inc="dv")
                ev_w1 = em.op("v", lambda v, wcol=wcol: v.tensor_scalar(wcol, m1_t[:], wn_t[:, 0:1], None, ALU.mult),
                              deps=[ev8], inc="dv")
                ev_v_w = em.op("v", lambda v, wcol=wcol: v.scalar_tensor_tensor(
                    wcol, m2_t[:], wn_t[:, 1:2], wcol, ALU.mult, ALU.add),
                    deps=[ev_w1], inc="dv")
            if Tr + 1 < n_tile:
                emit_h1fm_loads(Tr + 1)
            return ev_v_w, ("pe", em.cnt["pe"])

        if Ti == 0:
            ev_router_w, ev_pe_router_last = emit_router(0)
            ev_router_w_tile[0] = ev_router_w
        ev_router_w = ev_router_w_tile[Ti]

        for e in range(9):
            gsrc = sgS if e == 0 else wgS[e - 1]
            usrc = suS if e == 0 else wuS[e - 1]
            dsrc = sdS if e == 0 else wdS[e - 1]
            gi0 = (Ti * 9 + e) * N_IC

            wd_dep = []
            if (Ti, e) != (0, 0):
                prev = (Ti, e - 1) if e > 0 else (Ti - 1, 8)
                wd_dep = [ev_pe_y.get((prev[0], prev[1], 7), none_ev)]
            ev_wd = []

            for ic in range(N_IC):
                gi = gi0 + ic
                slot = gi % 3
                ring_dep = []
                if gi >= 3:
                    ring_dep = [ev_pe_u.get(gi - 3, none_ev)]
                ev_g_slab = em.dma("g", lambda g, gsrc=gsrc, ic=ic, slot=slot: g.dma_start(
                    out=wg_ring[slot][:], in_=gsrc[ic]),
                    deps=ring_dep + [ev_dg_gslab.get(gi - 3, none_ev)])
                ev_u_slab = em.dma("g", lambda g, usrc=usrc, ic=ic, slot=slot: g.dma_start(
                    out=wu_ring[slot][:], in_=usrc[ic]),
                    deps=ring_dep + [ev_dg_uslab.get(gi - 3, none_ev)])
                ev_dg_gslab[gi] = ev_g_slab
                ev_dg_uslab[gi] = ev_u_slab
                ev_wd.append(em.dma("g", lambda g, dsrc=dsrc, ic=ic: g.dma_start(
                    out=wd_all[:, ic * 1024:(ic + 1) * 1024], in_=dsrc[ic]),
                    deps=(wd_dep if ic == 0 else []) + [ev_wd_prev[ic]]))
                ev_wd_prev[ic] = ev_wd[-1]

                pe_deps = [ev_g_slab] + ev_h1fm_all
                if gi >= 1:
                    pe_deps.append(ev_a_silu.get(gi - 1, none_ev))
                if Ti == 0 and e == 0 and ic == 0:
                    pe_deps.append(ev_pe_router_last)
                em.wait("t", pe_deps)

                def g_mms(t, slot=slot, hb=h1fm_cur):
                    last = None
                    for hc in range(NHC):
                        for nn in range(2):
                            last = t.matmul(ps_a[:, nn * 512:(nn + 1) * 512],
                                     wg_ring[slot][:, hc * 128:(hc + 1) * 128],
                                     hb[:, hc * 1024 + nn * 512: hc * 1024 + (nn + 1) * 512],
                                     start=(hc == 0), stop=(hc == NHC - 1))
                    return last
                em.op("t", g_mms, inc="pe")
                ev_pe_g[gi] = ("pe", em.cnt["pe"])

                pe_deps = [ev_u_slab]
                if gi >= 1:
                    pe_deps.append(ev_v_hmul.get(gi - 1, none_ev))
                em.wait("t", pe_deps)

                def u_mms(t, slot=slot, hb=h1fm_cur):
                    last = None
                    for hc in range(NHC):
                        for nn in range(2):
                            last = t.matmul(ps_a[:, 1024 + nn * 512:1024 + (nn + 1) * 512],
                                     wu_ring[slot][:, hc * 128:(hc + 1) * 128],
                                     hb[:, hc * 1024 + nn * 512: hc * 1024 + (nn + 1) * 512],
                                     start=(hc == 0), stop=(hc == NHC - 1))
                    return last
                em.op("t", u_mms, inc="pe")
                ev_pe_u[gi] = ("pe", em.cnt["pe"])

                ev_a_silu[gi] = em.op("sc", lambda sc: sc.activation(sg_bf[:], ps_a[:, 0:1024], AF.Silu),
                                      deps=[ev_pe_g[gi]] + ([ev_v_hmul.get(gi - 1, none_ev)] if gi >= 1 else []),
                                      inc="ac")
                hdep = [ev_a_silu[gi], ev_pe_u[gi]]
                if (Ti, e) != (0, 0):
                    prev = (Ti, e - 1) if e > 0 else (Ti - 1, 8)
                    hdep.append(ev_pe_y.get((prev[0], prev[1], 7), none_ev))
                ev_v_hmul[gi] = em.op("v", lambda v, ic=ic: v.tensor_tensor(
                    h_e[:, ic * 1024:(ic + 1) * 1024], sg_bf[:], ps_a[:, 1024:2048], ALU.mult),
                    deps=hdep, inc="dv")
            ev_pe_u_done[Ti] = ev_pe_u[gi0 + N_IC - 1]
            ev_wd_all = Em.evmax(ev_wd)

            for tc in range(8):
                yb = tc % 2
                pe_deps = ([ev_v_hmul[gi0 + N_IC - 1]] if tc > 0 else []) + ev_wd_all
                prevkey = None
                if tc >= 2:
                    prevkey = (Ti, e, tc - 2)
                elif e > 0:
                    prevkey = (Ti, e - 1, 6 + tc)
                elif Ti > 0:
                    prevkey = (Ti - 1, 8, 6 + tc)
                if prevkey is not None:
                    pe_deps.append(ev_v_acc.get(prevkey, none_ev))
                em.wait("t", pe_deps)

                hm = [ev_v_hmul[gi0 + ic] for ic in range(N_IC)] if tc == 0 else None

                def y_mms(t, tc=tc, yb=yb, hm=hm):
                    last = None
                    for ic in range(N_IC):
                        if hm is not None:
                            t.wait_ge(em.sems["dv"], hm[ic][1])
                        for nn in range(2):
                            last = t.matmul(ps_b[:, yb * 1024 + nn * 512: yb * 1024 + (nn + 1) * 512],
                                     h_e[:, ic * 1024 + tc * 128: ic * 1024 + (tc + 1) * 128],
                                     wd_all[:, ic * 1024 + nn * 512: ic * 1024 + (nn + 1) * 512],
                                     start=(ic == 0), stop=(ic == N_IC - 1))
                    return last
                em.op("t", y_mms, inc="pe")
                ev_pe_y[(Ti, e, tc)] = ("pe", em.cnt["pe"])

                accblk = acc[:, tc * 1024:(tc + 1) * 1024]
                if e == 0:
                    adeps = [ev_pe_y[(Ti, e, tc)]]
                    if Ti > 0:
                        adeps.append(ev_v_LN2[Ti - 1][tc])
                    ev_v_acc[(Ti, e, tc)] = em.op(
                        "v", lambda v, accblk=accblk, yb=yb: v.tensor_copy(
                            accblk, ps_b[:, yb * 1024:(yb + 1) * 1024]),
                        deps=adeps, inc="dv")
                else:
                    ev_v_acc[(Ti, e, tc)] = em.op(
                        "v", lambda v, accblk=accblk, yb=yb, tc=tc, e=e: v.scalar_tensor_tensor(
                            accblk, ps_b[:, yb * 1024:(yb + 1) * 1024],
                            w_t[:, tc * NE + (e - 1): tc * NE + e], accblk,
                            ALU.mult, ALU.add),
                        deps=[ev_pe_y[(Ti, e, tc)], ev_router_w], inc="dv")

        if Ti + 1 < n_tile:
            ev_router_w_tile[Ti + 1], _ = emit_router(Ti + 1)

        for tc in range(8):
            ob = tc % 2
            accblk = acc[:, tc * 1024:(tc + 1) * 1024]
            deps = [ev_v_acc[(Ti, 8, tc)], ev_h1tm[tc]]
            ev_r2 = em.op("v", lambda v, accblk=accblk, tc=tc: v.tensor_add(
                r2_f[:], accblk, h1tm_t[:, tc * 1024:(tc + 1) * 1024]), deps=deps, inc="dv")
            ev_ns = em.op("v", lambda v: v.tensor_reduce(ln2[:, 0:1], r2_f[:], AX.X, ALU.add, negate=True),
                          inc="dv")
            ev_nm2 = em.op("v", lambda v: v.tensor_scalar(ln2[:, 1:2], ln2[:, 0:1], 1.0 / D, None, ALU.mult),
                           inc="dv")
            ev_rc2 = em.op("v", lambda v: v.tensor_scalar(rc2_f[:], r2_f[:], ln2[:, 1:2], None, ALU.add),
                           deps=[ev_nm2], inc="dv")
            ev_sq2 = em.op("sc", lambda sc: sc.activation(sq2[:], rc2_f[:], AF.Square,
                                                          accum_out=ln2[:, 2:3]),
                           deps=[ev_rc2], inc="ac")
            em.op("sc", lambda sc: sc.activation(ln2[:, 3:4], ln2[:, 2:3], AF.Ln,
                                                 scale=1.0 / D, bias=eps_t[:, 0:1]), inc="ac")
            ev_rs2 = em.op("sc", lambda sc: sc.activation(ln2[:, 4:5], ln2[:, 3:4], AF.Exp,
                                                          scale=-0.5), inc="ac")
            prev_out = None
            if tc >= 2:
                prev_out = ev_out_dma[Ti][tc - 2]
            elif Ti >= 1:
                prev_out = ev_out_dma[Ti - 1][tc + 6]
            ev_o = em.op("v", lambda v, ob=ob: v.tensor_scalar(out_t[ob][:], rc2_f[:], ln2[:, 4:5],
                                                               None, ALU.mult),
                         deps=[ev_rs2] + ([prev_out] if prev_out else []), inc="dv")
            ev_v_LN2[Ti][tc] = ev_o
            ev_out_dma[Ti][tc] = em.dma("sy", lambda sy, Ti=Ti, tc=tc, ob=ob: sy.dma_start(
                out=out_hbm[Ti * 1024 + tc * 128: Ti * 1024 + (tc + 1) * 128, :],
                in_=out_t[ob][:]), deps=[ev_o])

    em.wait("sy", em.fence("ds") + em.fence("dg") + em.fence("dc"))
    with nc.Block() as blkB:
        em.replay(blkB)
    phB.close()
    es.close()
    return nc


_prog_cache = {}


def _get_program():
    if "nc" not in _prog_cache:
        _prog_cache["nc"] = build_program()
    return _prog_cache["nc"]


def kernel(**inputs):
    f32 = np.float32
    x = np.asarray(inputs["L0_attn_x"], f32).reshape(32768, 1024)
    wqkv = np.ascontiguousarray(np.asarray(inputs["L0_attn_W_qkv"], f32))
    gamma = np.asarray(inputs["L0_attn_ln_gamma"], f32)
    beta = np.asarray(inputs["L0_attn_ln_beta"], f32)
    wproj = np.ascontiguousarray(np.asarray(inputs["L0_attn_W_proj"], f32))
    bproj = np.asarray(inputs["L0_attn_b_proj"], f32)
    sg = np.asarray(inputs["L0_ffn_sg_w"], f32)
    su = np.asarray(inputs["L0_ffn_su_w"], f32)
    sd = np.asarray(inputs["L0_ffn_sd_w"], f32)
    rw = np.asarray(inputs["L0_ffn_router_w"], f32)
    rb = np.asarray(inputs["L0_ffn_routing_bias"], f32)
    wg = np.asarray(inputs["L0_ffn_wg"], f32)
    wu = np.asarray(inputs["L0_ffn_wu"], f32)
    wd = np.asarray(inputs["L0_ffn_wd"], f32)

    C = np.ascontiguousarray
    gamma_t = C(np.broadcast_to(np.tile(gamma, NH), (128, D)))
    beta_t = C(np.broadcast_to(np.tile(beta, NH), (128, D)))
    bproj_t = C(np.broadcast_to(bproj, (128, D)))
    routerT = C(rw.T)
    rbias_t = C(np.broadcast_to(rb, (128, NE)))
    def slabs_IH(wm):
        return C(wm.reshape(N_IC, 128, NHC, 128).transpose(0, 3, 2, 1).reshape(N_IC, 128, D))

    def slabs_HI(wm):
        return C(wm.reshape(D, N_IC, 128).transpose(1, 2, 0))
    sgS = slabs_IH(sg)
    suS = slabs_IH(su)
    sdS = slabs_HI(sd)
    wgS = C(wg.reshape(NE, N_IC, 128, NHC, 128).transpose(0, 1, 4, 3, 2).reshape(NE, N_IC, 128, D))
    wuS = C(wu.reshape(NE, N_IC, 128, NHC, 128).transpose(0, 1, 4, 3, 2).reshape(NE, N_IC, 128, D))
    wdS = C(wd.reshape(NE, D, N_IC, 128).transpose(0, 2, 3, 1))
    ident = np.eye(128, dtype=f32)

    shared = dict(wqkv=wqkv, gamma_t=gamma_t, beta_t=beta_t, wproj=wproj,
                  bproj_t=bproj_t, routerT=routerT, rbias_t=rbias_t,
                  sgS=sgS, suS=suS, sdS=sdS, wgS=wgS, wuS=wuS, wdS=wdS,
                  ident_in=ident)
    in_maps = []
    for c in range(N_CORES):
        sl = x[c * TOK:(c + 1) * TOK]
        x_tm_c = C(sl)
        xS_c = C(sl.reshape(N_SUB, 128, NHC, 128).transpose(0, 3, 2, 1).reshape(N_SUB, 128, D))
        m = dict(shared)
        m["x_tm"] = x_tm_c
        m["xS"] = xS_c
        in_maps.append(m)

    nc = _get_program()
    res = run_bass_kernel_spmd(nc, in_maps, core_ids=list(range(N_CORES)))
    out = np.concatenate([res.results[c]["out"] for c in range(N_CORES)], axis=0)
    return out.reshape(4, 8192, 1024).astype(np.float32)
